# version 21
# baseline (speedup 1.0000x reference)
from contextlib import ExitStack
import math
import numpy as np
import ml_dtypes
import concourse.bass as bass
import concourse.mybir as mybir
from concourse.bass_utils import run_bass_kernel_spmd

F32 = mybir.dt.float32
BF16 = mybir.dt.bfloat16
I32 = mybir.dt.int32
AF = mybir.ActivationFunctionType
ALU = mybir.AluOpType

H = 4096
S = 4096
NB = 2
NH = 32
KVH = 8
HD = 128
CD = 512
II = 14336
NHC = H // 128
TRIPS = (32, 16)
EXP_SHIFT = -24.0
SM_SCALE = 1.0 / math.sqrt(HD)

ENGINES = ("sync", "scalar", "vector", "tensor", "gpsimd")
CONST_VALS = [1.0 / H, -1.0 / H, -1.0, 1e-5, -0.5, SM_SCALE, EXP_SHIFT]


class Plan:

    def __init__(self, nc, es):
        self.nc = nc
        self.es = es
        self.ops = {e: [] for e in ENGINES}
        self.sems = {}
        for name in ("dld", "dst", "dgp", "epe", "eact", "edve", "epool"):
            self._mk(name)

    def _mk(self, name):
        self.sems[name] = [self.es.enter_context(self.nc.semaphore(name)), 0]
        return self.sems[name]

    def ch(self, name):
        if name not in self.sems:
            self._mk(name)
        return name

    def emit(self, eng, fn):
        self.ops[eng].append(fn)

    def bump(self, sem, k):
        self.sems[sem][1] += k
        return self.sems[sem][1]

    def h(self, sem):
        return self.sems[sem][0]

    def n(self, sem):
        return self.sems[sem][1]

    def wait(self, eng, sem, val):
        if val and val > 0:
            self.emit(eng, lambda e, h=self.h(sem), v=val: e.wait_ge(h, v))

    def barrier(self, final=False):
        snap = [(nm, s[1]) for nm, s in self.sems.items()
                if final or not nm.startswith("wc_")]
        for eng in ENGINES:
            for nm, v in snap:
                self.wait(eng, nm, v)

    def load(self, dst, src, transpose=False, queue="sync", ch="dld"):
        ch = self.ch(ch)
        v = self.bump(ch, 16)
        if transpose:
            self.emit(queue, lambda e, d=dst, s=src, h=self.h(ch):
                      e.dma_start(out=d, in_=s, transpose=True).then_inc(h, 16))
        else:
            self.emit(queue, lambda e, d=dst, s=src, h=self.h(ch):
                      e.dma_start(out=d, in_=s).then_inc(h, 16))
        return v

    def store(self, dst, src, ch="dst"):
        ch = self.ch(ch)
        v = self.bump(ch, 16)
        self.emit("scalar", lambda e, d=dst, s=src, h=self.h(ch):
                  e.dma_start(out=d, in_=s).then_inc(h, 16))
        return v

    def mm(self, out, lhsT, rhs, start, stop, inc=False):
        if inc:
            v = self.bump("epe", 1)
            self.emit("tensor", lambda e, o=out, l=lhsT, r=rhs, st=start, sp=stop, h=self.h("epe"):
                      e.matmul(o, l, r, start=st, stop=sp).then_inc(h, 1))
            return v
        self.emit("tensor", lambda e, o=out, l=lhsT, r=rhs, st=start, sp=stop:
                  e.matmul(o, l, r, start=st, stop=sp))
        return None

    def act(self, fn, inc=False):
        if inc:
            v = self.bump("eact", 1)
            self.emit("scalar", lambda e, f=fn, h=self.h("eact"): f(e).then_inc(h, 1))
            return v
        self.emit("scalar", lambda e, f=fn: f(e))
        return None

    def dve(self, fn, inc=False):
        if inc:
            v = self.bump("edve", 1)
            self.emit("vector", lambda e, f=fn, h=self.h("edve"): f(e).then_inc(h, 1))
            return v
        self.emit("vector", lambda e, f=fn: f(e))
        return None


def build_program(dbg=False):
    nc = bass.Bass("TRN2", target_bir_lowering=False)
    es = ExitStack()

    def din(name, shape, dtype):
        return nc.dram_tensor(name, shape, dtype, kind="ExternalInput").ap()

    xseq = din("xseq", [S, H], F32)
    xq = din("xq", [1024, H], F32)
    wq = din("wq", [H, H], F32)
    wkr = din("wkr", [512, H], F32)
    wdk = din("wdk", [CD, H], F32)
    wupk = din("wupk", [512, CD], F32)
    wupv = din("wupv", [1024, CD], F32)
    wo = din("wo", [H, H], F32)
    gq = din("gq", [II, H], I32)
    gsi = din("gs", [II, 32], F32)
    uqi = din("uq", [II, H], I32)
    usi = din("us", [II, 32], F32)
    dqi = din("dq", [H, II], I32)
    dsi = din("ds", [H, 112], F32)
    consts_in = din("consts", [128, len(CONST_VALS)], F32)
    ident_in = din("ident", [128, 128], F32)
    cosq_in = din("cosq", [128, 1024], BF16)
    sinq_in = din("sinq", [128, 1024], BF16)
    cosk_in = din("cosk", [64, S], BF16)
    sink_in = din("sinkp", [64, S], BF16)
    masks_in = din("masks", [128, 32 * 512], BF16)
    esel_in = din("esel", [4, 512], F32)
    e32_in = din("e32", [128, H], BF16)
    zpadi = din("zpad", [II, 96], BF16)

    out_t = nc.dram_tensor("out", [1024, H], F32, kind="ExternalOutput").ap()
    if dbg:
        dbg_ckv = nc.dram_tensor("dbg_ckv", [128, 4 * S], BF16, kind="ExternalOutput").ap()
        dbg_kf = nc.dram_tensor("dbg_kf", [128, 8 * S], BF16, kind="ExternalOutput").ap()
        dbg_vb = nc.dram_tensor("dbg_vb", [128, 32 * 128], BF16, kind="ExternalOutput").ap()
        dbg_p0 = nc.dram_tensor("dbg_p0", [128, 512], BF16, kind="ExternalOutput").ap()
        dbg_l4 = nc.dram_tensor("dbg_l4", [128, 4], F32, kind="ExternalOutput").ap()
        dbg_rl = nc.dram_tensor("dbg_rl", [4, 128], F32, kind="ExternalOutput").ap()
        dbg_bc = nc.dram_tensor("dbg_bc", [128, 512], F32, kind="ExternalOutput").ap()
        dbg_w0 = nc.dram_tensor("dbg_w0", [128, 512], BF16, kind="ExternalOutput").ap()
        dbg_sg = nc.dram_tensor("dbg_sg", [128, 512], BF16, kind="ExternalOutput").ap()
        dbg_bcs = nc.dram_tensor("dbg_bcs", [128, 512], BF16, kind="ExternalOutput").ap()

    def dsc(name, shape, dtype):
        kind = "ExternalOutput" if dbg else "Internal"
        return nc.dram_tensor(name, shape, dtype, kind=kind).ap()

    xn_tok2 = dsc("xn_tok2", [32, S, 128], BF16)
    xnq_tok2 = dsc("xnq_tok2", [32, 1024, 128], BF16)
    wq_sc2 = dsc("wq_sc2", [32, H, 128], BF16)
    wo_sc2 = dsc("wo_sc2", [32, H, 128], BF16)
    wkr_sc2 = dsc("wkr_sc2", [32, 512, 128], BF16)
    wdk_sc2 = dsc("wdk_sc2", [32, CD, 128], BF16)
    wupk_sc2 = dsc("wupk_sc2", [4, 512, 128], BF16)
    wupv_sc2 = dsc("wupv_sc2", [4, 1024, 128], BF16)
    d_sc2 = dsc("d_sc2", [112, H, 128], BF16)
    gq_bf2 = dsc("gq_bf2", [32, II, 128], BF16)
    uq_bf2 = dsc("uq_bf2", [32, II, 128], BF16)
    gs_bf = dsc("gs_bf", [II, 128], BF16)
    us_bf = dsc("us_bf", [II, 128], BF16)
    dq_bf = dsc("dq_bf", [H, II], BF16)
    qsp = dsc("qsp", [H, 1024], BF16)
    asp = dsc("asp", [H, 1024], BF16)
    base = dsc("base", [1024, H], F32)
    inter = dsc("inter", [II, 1024], BF16)

    SB_BASE = 16512

    def sb_at(name, shape, dtype, off):
        return nc.alloc_sbuf_tensor_at(name, shape, dtype, offset=SB_BASE + off)

    A1 = 5984
    A2 = A1 + 65536
    A3 = A2 + 65536
    A4 = A3 + 32768
    A5 = A4 + 32768

    cc = sb_at("cc", [128, len(CONST_VALS)], F32, 0)
    ident = sb_at("ident", [128, 128], F32, 64)
    ones_bf = sb_at("ones_bf", [128, 1], BF16, 576)
    ones1 = sb_at("ones1", [1, 128], F32, 608)
    st_s1 = sb_at("st_s1", [128, 2], F32, 1120)
    st_s2 = sb_at("st_s2", [128, 2], F32, 1152)
    st_m2 = sb_at("st_m2", [128, 2], F32, 1184)
    st_var = sb_at("st_var", [128, 2], F32, 1216)
    st_rstd = sb_at("st_rstd", [128, 2], F32, 1248)
    st_nm = sb_at("st_nm", [128, 2], F32, 1280)
    st_nb = sb_at("st_nb", [128, 2], F32, 1312)
    t_l4 = sb_at("t_l4", [128, 4], F32, 1344)
    t_rl = sb_at("t_rl", [4, 128], F32, 1376)
    t_bc = sb_at("t_bc", [128, 512], F32, 1888)
    t_bc2 = sb_at("t_bc2", [128, 512], F32, 3936)
    esel = sb_at("esel", [4, 512], F32, 3936)

    ln_x = sb_at("ln_x", [128, 2 * H], F32, A1)
    ln_xn = sb_at("ln_xn", [128, 2 * H], BF16, A1 + 32768)
    ln_junk = sb_at("ln_junk", [128, H], BF16, A1 + 49152)
    wc_in = sb_at("wc_in", [128, 2 * H], F32, A2)
    wc_out = sb_at("wc_out", [128, 2 * H], BF16, A2 + 32768)
    xnfm = sb_at("xnfm", [128, 2 * 16 * 512], BF16, A1)
    wkrT = sb_at("wkrT", [128, 32 * 512], BF16, A1 + 32768)
    kfull = sb_at("kfull", [128, 8 * S], BF16, A2)
    ckv = sb_at("ckv", [128, 4 * S], BF16, A3)
    wdkT = sb_at("wdkT", [128, 32 * 512], BF16, A4)
    upkT = sb_at("upkT", [128, 4 * 512], BF16, A4)
    ropek = sb_at("ropek", [64, 4 * 512], BF16, A5)
    rsc = sb_at("rsc", [64, 2 * 512], F32, A5 + 4096)
    xnqfm = sb_at("xnqfm", [128, 32 * 1024], BF16, A1)
    wqT = sb_at("wqT", [128, 32 * 512], BF16, A4)
    cosq = sb_at("cosq", [128, 1024], BF16, A5)
    sinq = sb_at("sinq", [128, 1024], BF16, A5 + 2048)
    qstage = sb_at("qstage", [128, 2 * 512], BF16, A5 + 4096)
    masks = sb_at("masks", [128, 32 * 512], BF16, A4)
    vbuf = sb_at("vbuf", [128, 32 * 128], BF16, A1)
    qring = sb_at("qring", [128, 2 * 1024], BF16, A1 + 8192)
    pring = sb_at("pring", [128, 4 * 512], BF16, A1 + 12288)
    aout = sb_at("aout", [128, 2 * 512], BF16, A1 + 16384)
    upvT = sb_at("upvT", [128, 4 * 1024], BF16, A1 + 18432)
    abuf = sb_at("abuf", [128, 32 * 1024], BF16, A1)
    woT = sb_at("woT", [128, 2 * 32 * 512], BF16, A2)
    ph5x = sb_at("ph5x", [128, 2 * 512], F32, A5)
    ph5y = sb_at("ph5y", [128, 2 * 512], F32, A5 + 4096)
    xnqfm2 = sb_at("xnqfm2", [128, 32 * 1024], BF16, A1)
    wspan = sb_at("wspan", [128, 2 * 32 * 512], BF16, A2)
    e32 = sb_at("e32", [128, H], BF16, A3)
    sgts = sb_at("sgts", [128, 2 * 512], BF16, A3 + 8192)
    bc_sb = sb_at("bc_sb", [128, 2 * 512], BF16, A3 + 12288)
    dqraw = sb_at("dqraw", [128, 2 * H], BF16, A4)
    dqout = sb_at("dqout", [128, 2 * H], BF16, A4 + 16384)
    gsbuf = sb_at("gsbuf", [128, 8 * 512], BF16, A5)
    ffev = sb_at("ffev", [128, 4 * 512], BF16, 1888)
    acc = sb_at("acc", [128, 8 * H], F32, A1)
    wdT03 = sb_at("wdT03", [128, 4 * H], BF16, A3)
    i16 = sb_at("i16", [128, 7 * 1024], BF16, A4)
    wdT45 = sb_at("wdT45", [128, 2 * H], BF16, A4 + 14336)
    wdT6 = sb_at("wdT6", [128, H], BF16, A5)
    dsc_sb = sb_at("dsc_sb", [128, 3584], F32, A3 + 14336)
    ph7x = sb_at("ph7x", [128, 2 * 512], F32, 1888)

    for i, v in enumerate(CONST_VALS):
        nc.const_aps.register(F32, float(v), cc[:, i:i + 1])

    pb = [es.enter_context(nc.psum_tensor(f"pb{i}", [128, 512], F32)) for i in range(8)]

    pl = Plan(nc, es)

    pl.load(cc[:], consts_in)
    pl.load(ident[:], ident_in)
    pl.dve(lambda e: e.memset(ones_bf[:], 1.0))
    pl.dve(lambda e: e.memset(ones1[:], 1.0), inc=True)

    wc_tot = {}
    def cast_chunked(srcw, dstw, nrow, chn):
        step = 256 if nrow >= 256 else nrow
        for r0 in range(0, nrow, step):
            wc_tot[chn] = pl.load(dstw[:, r0:r0 + step, :],
                                  srcw[r0:r0 + step, :].rearrange("r (c k) -> c r k", k=128),
                                  queue="gpsimd", ch=chn)
    def cast_flat(srcw, dstw, nrow, chn):
        for r0 in range(0, nrow, 2048):
            wc_tot[chn] = pl.load(dstw[r0:r0 + 2048, :], srcw[r0:r0 + 2048, :],
                                  queue="gpsimd", ch=chn)
    kr_cast_jobs = [(wkr, wkr_sc2, 512), (wdk, wdk_sc2, CD),
                    (wupk, wupk_sc2, 512), (wupv, wupv_sc2, 1024)]
    def cast_chunked448(srcw, dstw, chn):
        for r0 in range(0, II, 448):
            wc_tot[chn] = pl.load(dstw[:, r0:r0 + 448, :],
                                  srcw[r0:r0 + 448, :].rearrange("r (c k) -> c r k", k=128),
                                  queue="gpsimd", ch=chn)

    ln_jobs = [(xseq[t * 128:(t + 1) * 128, :],
                xn_tok2[:, t * 128:(t + 1) * 128, :].rearrange("c t k -> t c k")) for t in range(32)]
    ln_jobs += [(xq[t * 128:(t + 1) * 128, :],
                 xnq_tok2[:, t * 128:(t + 1) * 128, :].rearrange("c t k -> t c k")) for t in range(8)]

    ln_st_v = [0, 0]
    ln_act_v = [0, 0]
    ln_tile_st = {}
    kr_emitted = [False]
    for j, (src, dst) in enumerate(ln_jobs):
        if j == 20 and not kr_emitted[0]:
            kr_emitted[0] = True
            pl.wait("gpsimd", "eact", pl.n("eact"))
            for a_, b_, n_ in kr_cast_jobs:
                cast_chunked(a_, b_, n_, "wc_kr")
        slot = j % 2
        xt = ln_x[:, slot * H:(slot + 1) * H]
        xo = ln_xn[:, slot * H:(slot + 1) * H]
        pl.wait("sync", f"lns{slot}", ln_st_v[slot])
        pl.wait("sync", "eact", ln_act_v[slot])
        lv = pl.load(xt, src, ch=f"lnx{slot}")
        s1 = st_s1[:, slot:slot + 1]
        s2 = st_s2[:, slot:slot + 1]
        m2 = st_m2[:, slot:slot + 1]
        var = st_var[:, slot:slot + 1]
        rstd = st_rstd[:, slot:slot + 1]
        nm = st_nm[:, slot:slot + 1]
        nb = st_nb[:, slot:slot + 1]
        pl.wait("vector", f"lnx{slot}", lv)
        dv1 = pl.dve(lambda e, xt=xt, s1=s1: e.tensor_scalar(
            out=ln_junk[:], in0=xt, scalar1=1.0, scalar2=None,
            op0=ALU.mult, op1=ALU.add, accum_out=s1), inc=True)
        pl.wait("scalar", f"lnx{slot}", lv)
        sq = pl.act(lambda e, xt=xt, s2=s2, xo=xo:
                    e.activation(out=xo, in_=xt, func=AF.Square, accum_out=s2), inc=True)
        pl.wait("vector", "edve", dv1)
        pl.wait("vector", "eact", sq)
        dv = pl.dve(lambda e, s1=s1, s2=s2, m2=m2, var=var, nm=nm:
                    (e.scalar_tensor_tensor(out=m2, in0=s1, scalar=1.0 / (H * float(H)),
                                            in1=s1, op0=ALU.mult, op1=ALU.mult),
                     e.tensor_scalar(out=var, in0=s2, scalar1=1.0 / H, scalar2=None, op0=ALU.mult),
                     e.tensor_scalar(out=nm, in0=s1, scalar1=-1.0 / H, scalar2=None, op0=ALU.mult))[2],
                    inc=True)
        pl.wait("vector", "edve", dv)
        dv = pl.dve(lambda e, var=var, m2=m2: e.tensor_sub(var, var, m2), inc=True)
        pl.wait("scalar", "edve", dv)
        av = pl.act(lambda e, var=var, rstd=rstd:
                    (e.activation(out=rstd, in_=var, func=AF.Ln, bias=1e-5),
                     e.activation(out=rstd, in_=rstd, func=AF.Exp, scale=-0.5))[1], inc=True)
        pl.wait("vector", "eact", av)
        dv = pl.dve(lambda e, nb=nb, nm=nm, rstd=rstd: e.tensor_mul(nb, nm, rstd), inc=True)
        pl.wait("scalar", "edve", dv)
        pl.wait("scalar", "eact", av)
        pl.wait("scalar", f"lnx{slot}", lv)
        av = pl.act(lambda e, xt=xt, xo=xo, rstd=rstd, nb=nb:
                    e.activation(out=xo, in_=xt, func=AF.Identity, scale=rstd, bias=nb), inc=True)
        ln_act_v[slot] = av
        pl.wait("scalar", "eact", av)
        ln_st_v[slot] = pl.store(dst, xo.rearrange("p (c k) -> p c k", k=128), ch=f"lns{slot}")
        ln_tile_st[j] = (f"lns{slot}", ln_st_v[slot])
    ln_done_v = pl.n("eact")
    pl.wait("gpsimd", "eact", max(0, ln_done_v - 6))
    cast_chunked(wq, wq_sc2, H, "wc_q")
    wc_tot["wc_d"] = pl.load(dq_bf[0:2048, :], dqi[0:2048, :], queue="gpsimd", ch="wc_d")
    wc_tot["wc_g"] = pl.load(gs_bf[:, 0:32], gsi[:], queue="gpsimd", ch="wc_g")
    wc_tot["wc_g"] = pl.load(gs_bf[:, 32:128], zpadi[:], queue="gpsimd", ch="wc_g")
    cast_chunked448(gq, gq_bf2, "wc_g")
    cast_chunked(wo, wo_sc2, H, "wc_o")
    wc_tot["wc_u"] = pl.load(us_bf[:, 0:32], usi[:], queue="gpsimd", ch="wc_u")
    wc_tot["wc_u"] = pl.load(us_bf[:, 32:128], zpadi[:], queue="gpsimd", ch="wc_u")
    cast_chunked448(uqi, uq_bf2, "wc_u")

    pl.wait("sync", "wc_kr", wc_tot["wc_kr"])
    pl.wait("sync", "eact", ln_done_v)
    wtv = 0
    for hc in range(NHC):
        wtv = pl.load(wdkT[:, hc * 512:(hc + 1) * 512], wdk_sc2[hc, :, :], transpose=True, ch="wt")
        wtv = pl.load(wkrT[:, hc * 512:(hc + 1) * 512], wkr_sc2[hc, :, :], transpose=True, ch="wt")
    pl.wait("tensor", "wt", wtv)

    half_pe = [0, 0]
    rope_done_sb = [0] * 8
    rope_done_sb2 = [0] * 8
    rope_scratch_free = [0]
    bank_free = {}
    kc_bank_free = {}
    for sb in range(8):
        slot2 = sb % 2
        hl_v = [0, 0]
        for h2 in range(2):
            buf = h2
            pl.wait("sync", "epe", half_pe[buf])
            for t in range(sb * 4, sb * 4 + 4):
                chn, v = ln_tile_st[t]
                pl.wait("sync", chn, v)
            lastv = 0
            for hc in range(h2 * 16, h2 * 16 + 16):
                lastv = pl.load(xnfm[:, buf * 8192 + (hc - h2 * 16) * 512: buf * 8192 + (hc - h2 * 16 + 1) * 512],
                                xn_tok2[hc, sb * 512:(sb + 1) * 512, :],
                                transpose=True, ch=f"xa{buf}")
            hl_v[h2] = lastv
        if sb >= 2:
            pl.wait("sync", "eact", rope_done_sb[sb - 2])
            pl.wait("sync", "edve", rope_done_sb2[sb - 2])
        pl.load(ropek[:, slot2 * 1024: slot2 * 1024 + 512], cosk_in[:, sb * 512:(sb + 1) * 512], ch=f"xa{0}")
        rv = pl.load(ropek[:, slot2 * 1024 + 512: slot2 * 1024 + 1024], sink_in[:, sb * 512:(sb + 1) * 512], ch=f"xa{0}")
        cosr = ropek[0:64, slot2 * 1024: slot2 * 1024 + 512]
        sinr = ropek[0:64, slot2 * 1024 + 512: slot2 * 1024 + 1024]
        pv_grp = [None] * 8
        for h2 in range(2):
            buf = h2
            pl.wait("tensor", f"xa{buf}", hl_v[h2])
            for grp in range(8):
                bank = pb[grp]
                if h2 == 0 and grp in bank_free:
                    pl.wait("tensor", bank_free[grp][0], bank_free[grp][1])
                wT = wdkT if grp < 4 else wkrT
                oc = grp % 4
                for hcl in range(16):
                    hc = h2 * 16 + hcl
                    lhsT = wT[:, hc * 512 + oc * 128: hc * 512 + (oc + 1) * 128]
                    rhs = xnfm[:, buf * 8192 + hcl * 512: buf * 8192 + (hcl + 1) * 512]
                    pv_grp[grp] = pl.mm(bank[:], lhsT, rhs, hc == 0, hc == NHC - 1,
                                        inc=(hc == NHC - 1))
            half_pe[buf] = pl.n("epe")
        for grp in range(4):
            pl.wait("scalar", "epe", pv_grp[grp])
            av = pl.act(lambda e, o=ckv[:, grp * S + sb * 512: grp * S + (sb + 1) * 512], i=pb[grp][:]:
                        e.activation(out=o, in_=i, func=AF.Identity), inc=True)
            bank_free[grp] = ("eact", av)
        for kc in range(4):
            grp = 4 + kc
            bank = pb[grp]
            pl.wait("scalar", "epe", pv_grp[grp])
            pl.wait("vector", f"xa{0}", rv)
            dvl = None
            avl = None
            for hh in range(2):
                g = 2 * kc + hh
                b0 = 64 * hh
                pl.wait("scalar", "edve", rope_scratch_free[0])
                avl = pl.act(lambda e, i=bank, b0=b0:
                             (e.activation(out=t_bc[0:64, :], in_=i[b0:b0 + 64, :], func=AF.Identity),
                              e.activation(out=t_bc2[0:32, :], in_=i[b0 + 32:b0 + 64, :], func=AF.Identity),
                              e.activation(out=t_bc2[32:64, :], in_=i[b0:b0 + 32, :], func=AF.Identity))[2],
                             inc=True)
                pl.wait("vector", "eact", avl)
                dv0 = pl.dve(lambda e, c=cosr, s=sinr:
                             (e.tensor_tensor(out=rsc[0:64, 0:512], in0=t_bc[0:64, :], in1=c, op=ALU.mult),
                              e.tensor_tensor(out=rsc[0:64, 512:1024], in0=t_bc2[0:64, :], in1=s, op=ALU.mult))[1],
                             inc=True)
                pl.wait("vector", "edve", dv0)
                dv1 = pl.dve(lambda e:
                             e.tensor_tensor(out=t_bc[0:64, :], in0=rsc[0:64, 0:512],
                                             in1=rsc[0:64, 512:1024], op=ALU.add), inc=True)
                pl.wait("vector", "edve", dv1)
                klo = kfull[0:32, g * S + sb * 512: g * S + sb * 512 + 512]
                khi = kfull[64:96, g * S + sb * 512: g * S + sb * 512 + 512]
                dvl = pl.dve(lambda e, klo=klo, khi=khi:
                             (e.tensor_copy(klo, t_bc[0:32, :]),
                              e.tensor_copy(khi, t_bc[32:64, :]))[1], inc=True)
                rope_scratch_free[0] = dvl
            bank_free[grp] = ("eact", avl)
        rope_done_sb[sb] = pl.n("eact")
        rope_done_sb2[sb] = pl.n("edve")

    pl.barrier()

    upv = 0
    for ccc in range(4):
        upv = pl.load(upkT[:, ccc * 512:(ccc + 1) * 512], wupk_sc2[ccc, :, :], transpose=True, ch="wt")
    pl.wait("tensor", "wt", upv)
    kc_bank_free = {}
    for g in range(KVH):
        for sb in range(8):
            bk = (g * 8 + sb) % 2
            bank = pb[bk]
            if bk in kc_bank_free:
                pl.wait("tensor", "eact", kc_bank_free[bk])
            pv = None
            for ccc in range(4):
                lhsT = upkT[:, ccc * 512 + g * 64: ccc * 512 + (g + 1) * 64]
                rhs = ckv[:, ccc * S + sb * 512: ccc * S + (sb + 1) * 512]
                pv = pl.mm(bank[0:64, :], lhsT, rhs, ccc == 0, ccc == 3, inc=(ccc == 3))
            pl.wait("scalar", "epe", pv)
            k1 = kfull[32:64, g * S + sb * 512: g * S + sb * 512 + 512]
            k2 = kfull[96:128, g * S + sb * 512: g * S + sb * 512 + 512]
            av = pl.act(lambda e, k1=k1, k2=k2, i=bank:
                        (e.activation(out=k1, in_=i[0:32, :], func=AF.Identity),
                         e.activation(out=k2, in_=i[32:64, :], func=AF.Identity))[1], inc=True)
            kc_bank_free[bk] = av

    pl.barrier()

    for hc in range(NHC):
        pl.load(xnqfm[:, hc * 1024:(hc + 1) * 1024], xnq_tok2[hc, :, :], transpose=True)
    pl.load(cosq[:], cosq_in)
    pl.load(sinq[:], sinq_in)
    pl.barrier()

    q_st_v = [0, 0]
    wq_used = 0
    q_scratch_free = [0]
    bank_free = {}
    for hg in range(8):
        pl.wait("sync", "epe", wq_used)
        if hg == 0:
            pl.wait("sync", "wc_q", wc_tot["wc_q"])
        lastv = 0
        for hc in range(NHC):
            lastv = pl.load(wqT[:, hc * 512:(hc + 1) * 512],
                            wq_sc2[hc, hg * 512:(hg + 1) * 512, :],
                            transpose=True, ch="wqt")
        pl.wait("tensor", "wqt", lastv)
        for hl in range(4):
            head = hg * 4 + hl
            for tb in range(2):
                bk = (head * 2 + tb) % 2
                bank = pb[bk]
                if bk in bank_free:
                    pl.wait("tensor", bank_free[bk][0], bank_free[bk][1])
                pv = None
                for hc in range(NHC):
                    lhsT = wqT[:, hc * 512 + hl * 128: hc * 512 + (hl + 1) * 128]
                    rhs = xnqfm[:, hc * 1024 + tb * 512: hc * 1024 + (tb + 1) * 512]
                    pv = pl.mm(bank[:], lhsT, rhs, hc == 0, hc == NHC - 1, inc=(hc == NHC - 1))
                qslot = (head * 2 + tb) % 2
                pl.wait("scalar", "epe", pv)
                pl.wait("scalar", "edve", q_scratch_free[0])
                av = pl.act(lambda e, i=bank:
                            (e.activation(out=t_bc[:], in_=i[:], func=AF.Identity),
                             e.activation(out=t_bc2[0:64, :], in_=i[64:128, :], func=AF.Identity),
                             e.activation(out=t_bc2[64:128, :], in_=i[0:64, :], func=AF.Identity))[2],
                            inc=True)
                pl.wait("vector", f"qs{qslot}", q_st_v[qslot])
                pl.wait("vector", "eact", av)
                ctab = cosq[:, tb * 512:(tb + 1) * 512]
                stab = sinq[:, tb * 512:(tb + 1) * 512]
                qs = qstage[:, qslot * 512:(qslot + 1) * 512]
                dv0 = pl.dve(lambda e, c=ctab, s=stab:
                             (e.tensor_tensor(out=t_bc[:], in0=t_bc[:], in1=c, op=ALU.mult),
                              e.tensor_tensor(out=t_bc2[:], in0=t_bc2[:], in1=s, op=ALU.mult))[1], inc=True)
                pl.wait("vector", "edve", dv0)
                dv1 = pl.dve(lambda e, qs=qs:
                             e.tensor_tensor(out=qs, in0=t_bc[:], in1=t_bc2[:], op=ALU.add), inc=True)
                q_scratch_free[0] = dv1
                bank_free[bk] = ("eact", av)
                pl.wait("scalar", "edve", dv1)
                q_st_v[qslot] = pl.store(qsp[head * 128:(head + 1) * 128, tb * 512:(tb + 1) * 512], qs,
                                         ch=f"qs{qslot}")
        wq_used = pl.n("epe")

    pl.barrier()

    pl.load(masks[:], masks_in)
    pl.load(esel[:], esel_in)
    pl.wait("gpsimd", "eact", pl.n("eact"))
    wc_tot["wc_d"] = pl.load(dq_bf[2048:4096, :], dqi[2048:4096, :], queue="gpsimd", ch="wc_d")
    for ccc in range(4):
        pl.load(upvT[:, ccc * 1024:(ccc + 1) * 1024], wupv_sc2[ccc, :, :], transpose=True)
    pl.barrier()

    ast_v = [0, 0]
    q_used = [0, 0]
    bank_last_exp = {0: 0, 1: 0}
    tail_aout_dve = {0: 0, 1: 0}
    tail_l4_dve = {0: 0, 1: 0}
    tail_recip = [0]
    tail_bc_av = [0]
    tail_esel_pe = [0]
    tail_tv = [0]
    for g in range(KVH):
        pl.wait("tensor", "edve", tail_recip[0])
        pl.wait("tensor", "eact", tail_bc_av[0])
        vb_free = {}
        for sc in range(32):
            bk = 4 + sc % 2
            bank = pb[bk]
            if bk in vb_free:
                pl.wait("tensor", "edve", vb_free[bk])
            pv = None
            for ccc in range(4):
                lhsT = ckv[:, ccc * S + sc * 128: ccc * S + (sc + 1) * 128]
                rhs = upvT[:, ccc * 1024 + g * 128: ccc * 1024 + (g + 1) * 128]
                pv = pl.mm(bank[:, 0:128], lhsT, rhs, ccc == 0, ccc == 3, inc=(ccc == 3))
            pl.wait("vector", "epe", pv)
            av = pl.dve(lambda e, o=vbuf[:, sc * 128:(sc + 1) * 128], i=bank[:, 0:128]:
                        e.tensor_copy(o, i), inc=True)
            vb_free[bk] = av
        vready = pl.n("edve")
        if dbg and g == 0:
            pl.wait("scalar", "edve", vready)
            pl.store(dbg_vb, vbuf[:], ch="dbgs")

        for hl in range(4):
            head = g * 4 + hl
            qslot = head % 2
            pl.wait("sync", "epe", q_used[qslot])
            qlv = pl.load(qring[:, qslot * 1024:(qslot + 1) * 1024], qsp[head * 128:(head + 1) * 128, :],
                          ch=f"qr{qslot}")
            pl.wait("tensor", f"qr{qslot}", qlv)
            pl.wait("tensor", "edve", vready)
            pl.wait("tensor", "edve", tail_aout_dve[0])
            pl.wait("tensor", "edve", tail_aout_dve[1])
            pl.wait("tensor", "edve", tail_l4_dve[0])
            pl.wait("tensor", "edve", tail_l4_dve[1])
            T0 = TRIPS[0]
            obank_ = {0: pb[2], 1: pb[6]}
            lbank_ = {0: pb[3], 1: pb[7]}
            p_ready = {}
            av_done = {}
            o_done = {}
            l_done = {}
            for k in range(T0 + 1):
                for tb in range(2):
                    T = TRIPS[tb]
                    kk = k - 1
                    if 0 <= kk < T:
                        semn, pval = p_ready[(tb, kk)]
                        pl.wait("tensor", semn, pval)
                        pslot = tb * 2 + kk % 2
                        ptile = pring[:, pslot * 512:(pslot + 1) * 512]
                        ov = pl.mm(obank_[tb][:], vbuf[:, kk * 128:(kk + 1) * 128], ptile,
                                   kk == 0, kk == T - 1, inc=(kk == T - 1))
                        if kk == T - 1:
                            o_done[tb] = ov
                        for sub in range(4):
                            lv_ = pl.mm(lbank_[tb][:, sub:sub + 1], ptile[:, sub * 128:(sub + 1) * 128],
                                        ones_bf[:], kk == 0 and sub == 0,
                                        kk == T - 1 and sub == 3,
                                        inc=(kk == T - 1 and sub == 3))
                            if kk == T - 1 and sub == 3:
                                l_done[tb] = lv_
                        av_done[(tb, kk)] = pl.n("epe")
                    if k < T:
                        b = tb if k < 16 else k % 2
                        if k < 2:
                            pl.wait("tensor", "eact", bank_last_exp[b])
                        lhsT = kfull[:, g * S + k * 128: g * S + (k + 1) * 128]
                        rhs = qring[:, qslot * 1024 + tb * 512: qslot * 1024 + (tb + 1) * 512]
                        sv = pl.mm(pb[b][:], lhsT, rhs, True, True, inc=True)
                        pl.wait("scalar", "epe", sv)
                        if k >= 2:
                            pl.wait("scalar", "epe", av_done.get((tb, k - 2), 0))
                        pslot = tb * 2 + k % 2
                        ptile = pring[:, pslot * 512:(pslot + 1) * 512]
                        av = pl.act(lambda e, o=ptile, i=pb[b][:]:
                                    e.activation(out=o, in_=i, func=AF.Exp, scale=SM_SCALE,
                                                 bias=EXP_SHIFT), inc=True)
                        bank_last_exp[b] = av
                        if dbg and g == 0 and hl == 0 and tb == 0 and k == 0:
                            pl.wait("scalar", "eact", av)
                            pl.store(dbg_p0, ptile, ch="dbgs")
                        masked = (tb == 1) or (k >= 16)
                        if masked:
                            mk = (16 + k) if tb == 1 else (k - 16)
                            pl.wait("vector", "eact", av)
                            dv = pl.dve(lambda e, o=ptile, m=masks[:, mk * 512:(mk + 1) * 512]:
                                        e.tensor_tensor(out=o, in0=o, in1=m, op=ALU.mult), inc=True)
                            p_ready[(tb, k)] = ("edve", dv)
                        else:
                            p_ready[(tb, k)] = ("eact", av)
            for tb in range(2):
                par = tb
                obank = obank_[tb]
                lbank = lbank_[tb]
                pl.wait("vector", "epe", l_done[tb])
                pl.wait("vector", "epe", tail_tv[0])
                dv = pl.dve(lambda e, i=lbank[:, 0:4]: e.tensor_copy(t_l4[:], i), inc=True)
                tail_l4_dve[tb] = dv
                if dbg and g == 0 and hl == 0 and tb == 0:
                    pl.wait("scalar", "edve", dv)
                    pl.store(dbg_l4, t_l4[:], ch="dbgs")
                pl.wait("tensor", "edve", dv)
                pl.wait("tensor", "edve", tail_recip[0])
                tv = pl.bump("epe", 1)
                pl.emit("tensor", lambda e, h=pl.h("epe"):
                        e.transpose(pb[4][0:4, 0:128], t_l4[:], ident[:]).then_inc(h, 1))
                tail_tv[0] = tv
                pl.wait("vector", "epe", tv)
                pl.wait("vector", "epe", tail_esel_pe[0])
                dv = pl.dve(lambda e: e.reciprocal(t_rl[:], pb[4][0:4, 0:128]), inc=True)
                tail_recip[0] = dv
                pl.wait("tensor", "edve", dv)
                pl.wait("tensor", "eact", tail_bc_av[0])
                bcv = None
                for sub in range(4):
                    bcv = pl.mm(pb[5][:, sub * 128:(sub + 1) * 128],
                                esel[:, sub * 128:(sub + 1) * 128], t_rl[0:4, 0:128],
                                sub == 0, sub == 3, inc=(sub == 3))
                tail_esel_pe[0] = bcv
                if dbg and g == 0 and hl == 0 and tb == 0:
                    pl.wait("scalar", "edve", pl.n("edve"))
                    pl.store(dbg_rl, t_rl[:], ch="dbgs")
                pl.wait("scalar", "epe", bcv)
                av = pl.act(lambda e: e.activation(out=t_bc[:], in_=pb[5][:], func=AF.Identity), inc=True)
                tail_bc_av[0] = av
                aslot = par
                pl.wait("vector", f"as{aslot}", ast_v[aslot])
                pl.wait("vector", "eact", av)
                pl.wait("vector", "epe", o_done[tb])
                if dbg and g == 0 and hl == 0 and tb == 0:
                    pl.wait("scalar", "eact", av)
                    pl.store(dbg_bc, t_bc[:], ch="dbgs")
                at = aout[:, aslot * 512:(aslot + 1) * 512]
                dv = pl.dve(lambda e, at=at, ob=obank[:]:
                            e.tensor_tensor(out=at, in0=ob, in1=t_bc[:], op=ALU.mult), inc=True)
                tail_aout_dve[par] = dv
                pl.wait("scalar", "edve", dv)
                ast_v[aslot] = pl.store(asp[head * 128:(head + 1) * 128, tb * 512:(tb + 1) * 512], at,
                                        ch=f"as{aslot}")
            q_used[qslot] = pl.n("epe")

    pl.barrier()


    dq_jobs = []
    for hc in range(32):
        for qtr in range(8):
            dq_jobs.append(("d", hc, qtr))
    dq_done_v = {}
    dq_state = [0]
    dq_dsl = [None]
    dq_raw_free = [None] * 4
    dq_out_free = [0] * 4

    def emit_dequant(njobs):
        for _ in range(njobs):
            if dq_state[0] >= len(dq_jobs):
                return
            which, ic, hf = dq_jobs[dq_state[0]]
            dq_state[0] += 1
            slot = dq_state[0] % 4
            if dq_dsl[0] is None:
                dq_dsl[0] = pl.load(dsc_sb[:].rearrange("p (n g) -> p n g", g=112),
                                    dsi.rearrange("(n p) g -> p n g", p=128), ch="sc")
            scl_ap = dsc_sb[:, ic * 112 + hf * 14: ic * 112 + (hf + 1) * 14]
            ng = 14
            raw = dqraw[:, slot * 2048: slot * 2048 + 1792]
            outb = dqout[:, slot * 2048: slot * 2048 + 1792]
            srcap = dq_bf[ic * 128:(ic + 1) * 128, hf * 1792:(hf + 1) * 1792]
            dst = d_sc2[hf * 14:(hf + 1) * 14, ic * 128:(ic + 1) * 128, :].rearrange("c h k -> h c k")
            if dq_raw_free[slot]:
                pl.wait("sync", dq_raw_free[slot][0], dq_raw_free[slot][1])
            pl.wait("sync", "wc_" + which, wc_tot["wc_" + which])
            gv = pl.load(raw, srcap, ch=f"dr{slot}")
            scv = dq_dsl[0]
            pl.wait("vector", f"dr{slot}", gv)
            pl.wait("vector", "sc", scv)
            pl.wait("vector", f"dqs{slot}", dq_out_free[slot])
            dv = pl.dve(lambda e, o=outb, i=raw, s=scl_ap, ng=ng:
                        e.tensor_tensor(out=o.rearrange("p (g k) -> p g k", k=128),
                                        in0=i.rearrange("p (g k) -> p g k", k=128),
                                        in1=s.to_broadcast([128, ng, 128]),
                                        op=ALU.mult), inc=True)
            dq_raw_free[slot] = ("edve", dv)
            pl.wait("scalar", "edve", dv)
            stv = pl.store(dst, outb.rearrange("p (c k) -> p c k", k=128), ch=f"dqs{slot}")
            dq_out_free[slot] = stv

    e32_lv = pl.load(e32[:], e32_in, ch="sc")
    emit_dequant(8)
    abv = 0
    for dc in range(NHC):
        abv = pl.load(abuf[:, dc * 1024:(dc + 1) * 1024], asp[dc * 128:(dc + 1) * 128, :], ch="ab")
    pl.wait("tensor", "ab", abv)
    wo_used = [0, 0]
    bank_free = {}
    y_dve = [0, 0]
    y_st = [0, 0]
    for hs in range(8):
        slot = hs % 2
        if hs == 0:
            pl.wait("sync", "wc_o", wc_tot["wc_o"])
        pl.wait("sync", "epe", wo_used[slot])
        lastv = 0
        for dc in range(NHC):
            lastv = pl.load(woT[:, slot * 16384 + dc * 512: slot * 16384 + (dc + 1) * 512],
                            wo_sc2[dc, hs * 512:(hs + 1) * 512, :],
                            transpose=True, ch=f"wo{slot}")
        pl.wait("tensor", f"wo{slot}", lastv)
        for tc in range(8):
            bk = tc % 2
            bank = pb[bk]
            if bk in bank_free:
                pl.wait("tensor", "edve", bank_free[bk])
            pv = None
            for dc in range(NHC):
                lhsT = abuf[:, dc * 1024 + tc * 128: dc * 1024 + (tc + 1) * 128]
                rhs = woT[:, slot * 16384 + dc * 512: slot * 16384 + (dc + 1) * 512]
                pv = pl.mm(bank[:], lhsT, rhs, dc == 0, dc == NHC - 1, inc=(dc == NHC - 1))
            emit_dequant(2)
            xslot = tc % 2
            pl.wait("sync", "edve", y_dve[xslot])
            xv = pl.load(ph5x[:, xslot * 512:(xslot + 1) * 512],
                         xq[tc * 128:(tc + 1) * 128, hs * 512:(hs + 1) * 512], ch=f"px{xslot}")
            pl.wait("vector", f"px{xslot}", xv)
            pl.wait("vector", "epe", pv)
            pl.wait("vector", f"bs{xslot}", y_st[xslot])
            yt = ph5y[:, xslot * 512:(xslot + 1) * 512]
            dv = pl.dve(lambda e, yt=yt, a=bank[:], b=ph5x[:, xslot * 512:(xslot + 1) * 512]:
                        e.tensor_tensor(out=yt, in0=a, in1=b, op=ALU.add), inc=True)
            y_dve[xslot] = dv
            bank_free[bk] = dv
            pl.wait("scalar", "edve", dv)
            y_st[xslot] = pl.store(base[tc * 128:(tc + 1) * 128, hs * 512:(hs + 1) * 512], yt,
                                   ch=f"bs{xslot}")
        wo_used[slot] = pl.n("epe")

    pl.barrier()

    xnq2_lv = 0
    for hc in range(NHC):
        xnq2_lv = pl.load(xnqfm2[:, hc * 1024:(hc + 1) * 1024], xnq_tok2[hc, :, :],
                          transpose=True, ch="xn2")
    span_used = [0, 0]
    gs_act = {}
    ff_st = [0, 0, 0, 0]
    gbank_free = {}
    ubank_free = {}
    sg_free = [0, 0]
    bc_free = {}
    bcsb_free = [0, 0]
    for sp_ in range(28):
        for which in ("g", "u"):
            slot = (sp_ * 2 + (0 if which == "g" else 1)) % 2
            emit_dequant(5)
            if sp_ == 0:
                pl.wait("sync", "wc_" + which, wc_tot["wc_" + which])
            pl.wait("sync", "epe", span_used[slot])
            src_sc = gq_bf2 if which == "g" else uq_bf2
            src_scale = gs_bf if which == "g" else us_bf
            sslot = slot
            pl.wait("sync", "epe", sg_free[sslot])
            sgv = pl.load(sgts[:, sslot * 512:(sslot + 1) * 512],
                          src_scale[sp_ * 512:(sp_ + 1) * 512, :], transpose=True, ch=f"sg{sslot}")
            ld_v = []
            for hc in range(NHC):
                ld_v.append(pl.load(wspan[:, slot * 16384 + hc * 512: slot * 16384 + (hc + 1) * 512],
                                    src_sc[hc, sp_ * 512:(sp_ + 1) * 512, :],
                                    transpose=True, ch=f"ws{slot}"))
            tt_last = 0
            for hc in range(NHC):
                bk = 6 + hc % 2
                if bk in bc_free:
                    pl.wait("tensor", "eact", bc_free[bk])
                pl.wait("tensor", f"sg{sslot}", sgv)
                if sp_ == 0 and hc == 0:
                    pl.wait("tensor", "sc", e32_lv)
                bmv = pl.mm(pb[bk][:], e32[:, hc * 128:(hc + 1) * 128],
                            sgts[:, sslot * 512:(sslot + 1) * 512], True, True, inc=True)
                pl.wait("scalar", "epe", bmv)
                pl.wait("scalar", "edve", bcsb_free[hc % 2])
                av = pl.act(lambda e, o=bc_sb[:, (hc % 2) * 512:((hc % 2) + 1) * 512], i=pb[bk][:]:
                            e.activation(out=o, in_=i, func=AF.Identity), inc=True)
                bc_free[bk] = av
                wtile = wspan[:, slot * 16384 + hc * 512: slot * 16384 + (hc + 1) * 512]
                pl.wait("vector", f"ws{slot}", ld_v[hc])
                pl.wait("vector", "eact", av)
                tt_last = pl.dve(lambda e, w=wtile, b=bc_sb[:, (hc % 2) * 512:((hc % 2) + 1) * 512]:
                                 e.tensor_tensor(out=w, in0=w, in1=b, op=ALU.mult), inc=True)
                bcsb_free[hc % 2] = tt_last
            sg_free[sslot] = pl.n("epe")
            if dbg and sp_ == 0 and which == "g":
                pl.wait("scalar", "edve", tt_last)
                pl.store(dbg_w0, wspan[:, 0:512], ch="dbgs")
                pl.store(dbg_sg, sgts[:, 0:512], ch="dbgs")
                pl.store(dbg_bcs, bc_sb[:, 512:1024], ch="dbgs")
            pl.wait("tensor", "edve", tt_last)
            pl.wait("tensor", "xn2", xnq2_lv)
            for isub in range(4):
                for tb in range(2):
                    pi = isub * 2 + tb
                    if which == "g":
                        bk = pi % 4
                        bank = pb[bk]
                        if bk in gbank_free:
                            pl.wait("tensor", "eact", gbank_free[bk])
                    else:
                        bk = 4 + pi % 2
                        bank = pb[bk]
                        if bk in ubank_free:
                            pl.wait("tensor", "edve", ubank_free[bk])
                    pv = None
                    for hc in range(NHC):
                        lhsT = wspan[:, slot * 16384 + hc * 512 + isub * 128:
                                     slot * 16384 + hc * 512 + (isub + 1) * 128]
                        rhs = xnqfm2[:, hc * 1024 + tb * 512: hc * 1024 + (tb + 1) * 512]
                        pv = pl.mm(bank[:], lhsT, rhs, hc == 0, hc == NHC - 1, inc=(hc == NHC - 1))
                    if which == "g":
                        pl.wait("scalar", "epe", pv)
                        av = pl.act(lambda e, o=gsbuf[:, pi * 512:(pi + 1) * 512], i=bank[:]:
                                    e.activation(out=o, in_=i, func=AF.Silu), inc=True)
                        gs_act[(sp_, pi)] = av
                        gbank_free[bk] = av
                    else:
                        fslot = pi % 4
                        pl.wait("vector", f"fs{fslot}", ff_st[fslot])
                        pl.wait("vector", "epe", pv)
                        pl.wait("vector", "eact", gs_act[(sp_, pi)])
                        ft = ffev[:, fslot * 512:(fslot + 1) * 512]
                        dv = pl.dve(lambda e, ft=ft, a=bank[:], b=gsbuf[:, pi * 512:(pi + 1) * 512]:
                                    e.tensor_tensor(out=ft, in0=a, in1=b, op=ALU.mult), inc=True)
                        ubank_free[bk] = dv
                        pl.wait("scalar", "edve", dv)
                        ff_st[fslot] = pl.store(
                            inter[sp_ * 512 + isub * 128: sp_ * 512 + (isub + 1) * 128,
                                  tb * 512:(tb + 1) * 512], ft, ch=f"fs{fslot}")
            span_used[slot] = pl.n("epe")

    emit_dequant(10 ** 6)
    pl.barrier()

    def wdtile(ic):
        if ic < 4:
            return wdT03[:, ic * H:(ic + 1) * H]
        if ic < 6:
            return wdT45[:, (ic - 4) * H:(ic - 4 + 1) * H]
        return wdT6[:, :]

    acc_dve = {}
    out_st = [0, 0]
    prev_evac = {0: 0, 1: 0}
    prev_pp_pe = 0
    for pp in range(16):
        pl.wait("sync", "epe", prev_pp_pe)
        ilv = 0
        for ic in range(7):
            row = pp * 896 + ic * 128
            ilv = pl.load(i16[:, ic * 1024:(ic + 1) * 1024], inter[row:row + 128, :], ch="i16")
        wlv = 0
        for ic in range(7):
            wlv = pl.load(wdtile(ic), d_sc2[pp * 7 + ic, :, :], transpose=True, ch="wdt")
        pl.wait("tensor", "i16", ilv)
        pl.wait("tensor", "wdt", wlv)
        for hs in range(8):
            for half in range(2):
                tcs = range(4) if half == 0 else range(4, 8)
                pl.wait("tensor", "edve", prev_evac[half])
                lastpe = 0
                for ic in range(7):
                    rhs = wdtile(ic)[:, hs * 512:(hs + 1) * 512]
                    for tc in tcs:
                        lhsT = i16[:, ic * 1024 + tc * 128: ic * 1024 + (tc + 1) * 128]
                        lastpe = pl.mm(pb[tc][:], lhsT, rhs, ic == 0, ic == 6,
                                       inc=(tc == max(tcs)))
                pe_done = lastpe
                for tc in tcs:
                    aslice = acc[:, tc * H + hs * 512: tc * H + (hs + 1) * 512]
                    pl.wait("vector", "epe", pe_done)
                    if pp == 0:
                        bslot = tc % 2
                        prevb = acc_dve.get((hs, tc - 2), 0) if tc >= 2 else acc_dve.get((hs - 1, tc + 6), 0)
                        pl.wait("sync", "edve", prevb)
                        bv = pl.load(ph7x[:, bslot * 512:(bslot + 1) * 512],
                                     base[tc * 128:(tc + 1) * 128, hs * 512:(hs + 1) * 512], ch=f"pb7{bslot}")
                        pl.wait("vector", f"pb7{bslot}", bv)
                        dv = pl.dve(lambda e, o=aslice, a=pb[tc][:], b=ph7x[:, bslot * 512:(bslot + 1) * 512]:
                                    e.tensor_tensor(out=o, in0=a, in1=b, op=ALU.add), inc=True)
                    elif pp < 15:
                        pl.wait("vector", "edve", acc_dve[(hs, tc)])
                        dv = pl.dve(lambda e, o=aslice, a=pb[tc][:]:
                                    e.tensor_tensor(out=o, in0=a, in1=o, op=ALU.add), inc=True)
                    else:
                        pl.wait("vector", "edve", acc_dve[(hs, tc)])
                        oslot = tc % 2
                        pl.wait("vector", f"os{oslot}", out_st[oslot])
                        ot = ph7x[:, oslot * 512:(oslot + 1) * 512]
                        dv = pl.dve(lambda e, ot=ot, a=pb[tc][:], b=aslice:
                                    e.tensor_tensor(out=ot, in0=a, in1=b, op=ALU.add), inc=True)
                        pl.wait("scalar", "edve", dv)
                        out_st[oslot] = pl.store(out_t[tc * 128:(tc + 1) * 128, hs * 512:(hs + 1) * 512], ot,
                                                 ch=f"os{oslot}")
                    acc_dve[(hs, tc)] = dv
                prev_evac[half] = pl.n("edve")
        prev_pp_pe = pl.n("epe")

    pl.barrier(final=True)

    with nc.Block() as block:
        @block.sync
        def _(sync):
            for f in pl.ops["sync"]:
                f(sync)

        @block.scalar
        def _(scalar):
            for f in pl.ops["scalar"]:
                f(scalar)

        @block.vector
        def _(vector):
            for f in pl.ops["vector"]:
                f(vector)

        @block.tensor
        def _(tensor):
            for f in pl.ops["tensor"]:
                f(tensor)

        @block.gpsimd
        def _(gpsimd):
            for f in pl.ops["gpsimd"]:
                f(gpsimd)

    es.close()
    return nc


_CACHE = {}


def _host_constants(o, seq):
    jA = 4 + o
    jB = 3 - o
    pos = np.concatenate([np.arange(jA * 512, (jA + 1) * 512),
                          np.arange(jB * 512, (jB + 1) * 512)]).astype(np.float64)
    inv = 1.0 / (10000.0 ** (np.arange(0, HD, 2, dtype=np.float64) / HD))
    ang = pos[None, :] * inv[(np.arange(128) % 64)][:, None]
    cosq = np.cos(ang).astype(ml_dtypes.bfloat16)
    sq = np.sin(ang)
    sinq = np.empty((128, 1024), dtype=np.float64)
    sinq[0:64] = -sq[0:64]
    sinq[64:128] = sq[64:128]
    sinq = sinq.astype(ml_dtypes.bfloat16)
    posk = np.arange(S, dtype=np.float64)
    angk = posk[None, :] * inv[(np.arange(64) % 32)][:, None]
    cosk = np.cos(angk).astype(ml_dtypes.bfloat16)
    sk = np.sin(angk)
    sinkp = np.empty((64, S), dtype=np.float64)
    sinkp[0:32] = -sk[0:32]
    sinkp[32:64] = sk[32:64]
    sinkp = sinkp.astype(ml_dtypes.bfloat16)
    masks = np.zeros((128, 32, 512), dtype=np.float32)
    for idx in range(32):
        if idx < 16:
            k = 16 + idx
            j = 4 + o
        else:
            k = idx - 16
            j = 3 - o
        qpos = j * 512 + np.arange(512)[None, :]
        kpos = k * 128 + np.arange(128)[:, None]
        masks[:, idx, :] = (kpos <= qpos).astype(np.float32)
    masks = masks.reshape(128, 32 * 512).astype(ml_dtypes.bfloat16)
    consts = np.tile(np.array(CONST_VALS, dtype=np.float32)[None, :], (128, 1))
    ident = np.eye(128, dtype=np.float32)
    esel = np.zeros((4, 512), dtype=np.float32)
    for sub in range(4):
        esel[sub, sub * 128:(sub + 1) * 128] = 1.0
    e32 = np.zeros((128, 4096), dtype=ml_dtypes.bfloat16)
    for g in range(32):
        e32[g, g * 128:(g + 1) * 128] = 1.0
    zpad = np.zeros((II, 96), dtype=ml_dtypes.bfloat16)
    return dict(cosq=cosq, sinq=sinq, cosk=cosk, sinkp=sinkp, masks=masks,
                consts=consts, ident=ident, esel=esel, e32=e32, zpad=zpad)


def kernel(L0_attn_hidden_states, L0_attn_Wq, L0_attn_Wkr, L0_attn_Wdk,
           L0_attn_Wupk, L0_attn_Wupv, L0_attn_Wo,
           L0_ffn_gate_q, L0_ffn_gate_s, L0_ffn_up_q, L0_ffn_up_s,
           L0_ffn_down_q, L0_ffn_down_s):
    x = np.ascontiguousarray(np.asarray(L0_attn_hidden_states, dtype=np.float32))
    if "nc" not in _CACHE:
        _CACHE["nc"] = build_program()
    nc = _CACHE["nc"]

    shared = {
        "wq": np.ascontiguousarray(np.asarray(L0_attn_Wq, dtype=np.float32)),
        "wkr": np.ascontiguousarray(np.asarray(L0_attn_Wkr, dtype=np.float32)),
        "wdk": np.ascontiguousarray(np.asarray(L0_attn_Wdk, dtype=np.float32)),
        "wupk": np.ascontiguousarray(np.asarray(L0_attn_Wupk, dtype=np.float32)),
        "wupv": np.ascontiguousarray(np.asarray(L0_attn_Wupv, dtype=np.float32)),
        "wo": np.ascontiguousarray(np.asarray(L0_attn_Wo, dtype=np.float32)),
        "gq": np.ascontiguousarray(np.asarray(L0_ffn_gate_q, dtype=np.int32)),
        "gs": np.ascontiguousarray(np.asarray(L0_ffn_gate_s, dtype=np.float32)),
        "uq": np.ascontiguousarray(np.asarray(L0_ffn_up_q, dtype=np.int32)),
        "us": np.ascontiguousarray(np.asarray(L0_ffn_up_s, dtype=np.float32)),
        "dq": np.ascontiguousarray(np.asarray(L0_ffn_down_q, dtype=np.int32)),
        "ds": np.ascontiguousarray(np.asarray(L0_ffn_down_s, dtype=np.float32)),
    }
    in_maps = []
    for c in range(8):
        o = c % 4
        seq = c // 4
        jA = 4 + o
        jB = 3 - o
        m = dict(shared)
        m["xseq"] = np.ascontiguousarray(x[seq])
        m["xq"] = np.ascontiguousarray(
            np.concatenate([x[seq, jA * 512:(jA + 1) * 512], x[seq, jB * 512:(jB + 1) * 512]], axis=0))
        m.update(_host_constants(o, seq))
        in_maps.append(m)

    res = run_bass_kernel_spmd(nc, in_maps, core_ids=list(range(8)))
    out = np.empty((NB, S, H), dtype=np.float32)
    for c in range(8):
        o = c % 4
        seq = c // 4
        jA = 4 + o
        jB = 3 - o
        r = np.asarray(res.results[c]["out"])
        out[seq, jA * 512:(jA + 1) * 512] = r[0:512]
        out[seq, jB * 512:(jB + 1) * 512] = r[512:1024]
    return out


# version 23
# speedup vs baseline: 1.0496x; 1.0496x over previous
from contextlib import ExitStack
import math
import numpy as np
import ml_dtypes
import concourse.bass as bass
import concourse.mybir as mybir
from concourse.bass_utils import run_bass_kernel_spmd

F32 = mybir.dt.float32
BF16 = mybir.dt.bfloat16
I32 = mybir.dt.int32
AF = mybir.ActivationFunctionType
ALU = mybir.AluOpType

H = 4096
S = 4096
NB = 2
NH = 32
KVH = 8
HD = 128
CD = 512
II = 14336
NHC = H // 128
TRIPS = (32, 16)
EXP_SHIFT = -24.0
SM_SCALE = 1.0 / math.sqrt(HD)

ENGINES = ("sync", "scalar", "vector", "tensor", "gpsimd")
CONST_VALS = [1.0 / H, -1.0 / H, -1.0, 1e-5, -0.5, SM_SCALE, EXP_SHIFT]


class Plan:

    def __init__(self, nc, es):
        self.nc = nc
        self.es = es
        self.ops = {e: [] for e in ENGINES}
        self.sems = {}
        for name in ("dld", "dst", "dgp", "epe", "eact", "edve", "epool"):
            self._mk(name)

    def _mk(self, name):
        self.sems[name] = [self.es.enter_context(self.nc.semaphore(name)), 0]
        return self.sems[name]

    def ch(self, name):
        if name not in self.sems:
            self._mk(name)
        return name

    def emit(self, eng, fn):
        self.ops[eng].append(fn)

    def bump(self, sem, k):
        self.sems[sem][1] += k
        return self.sems[sem][1]

    def h(self, sem):
        return self.sems[sem][0]

    def n(self, sem):
        return self.sems[sem][1]

    def wait(self, eng, sem, val):
        if val and val > 0:
            self.emit(eng, lambda e, h=self.h(sem), v=val: e.wait_ge(h, v))

    def barrier(self, final=False):
        snap = [(nm, s[1]) for nm, s in self.sems.items()
                if final or not nm.startswith("wc_")]
        for eng in ENGINES:
            for nm, v in snap:
                self.wait(eng, nm, v)

    def load(self, dst, src, transpose=False, queue="sync", ch="dld"):
        ch = self.ch(ch)
        v = self.bump(ch, 16)
        if transpose:
            self.emit(queue, lambda e, d=dst, s=src, h=self.h(ch):
                      e.dma_start(out=d, in_=s, transpose=True).then_inc(h, 16))
        else:
            self.emit(queue, lambda e, d=dst, s=src, h=self.h(ch):
                      e.dma_start(out=d, in_=s).then_inc(h, 16))
        return v

    def store(self, dst, src, ch="dst"):
        ch = self.ch(ch)
        v = self.bump(ch, 16)
        self.emit("scalar", lambda e, d=dst, s=src, h=self.h(ch):
                  e.dma_start(out=d, in_=s).then_inc(h, 16))
        return v

    def mm(self, out, lhsT, rhs, start, stop, inc=False):
        if inc:
            v = self.bump("epe", 1)
            self.emit("tensor", lambda e, o=out, l=lhsT, r=rhs, st=start, sp=stop, h=self.h("epe"):
                      e.matmul(o, l, r, start=st, stop=sp).then_inc(h, 1))
            return v
        self.emit("tensor", lambda e, o=out, l=lhsT, r=rhs, st=start, sp=stop:
                  e.matmul(o, l, r, start=st, stop=sp))
        return None

    def act(self, fn, inc=False):
        if inc:
            v = self.bump("eact", 1)
            self.emit("scalar", lambda e, f=fn, h=self.h("eact"): f(e).then_inc(h, 1))
            return v
        self.emit("scalar", lambda e, f=fn: f(e))
        return None

    def dve(self, fn, inc=False):
        if inc:
            v = self.bump("edve", 1)
            self.emit("vector", lambda e, f=fn, h=self.h("edve"): f(e).then_inc(h, 1))
            return v
        self.emit("vector", lambda e, f=fn: f(e))
        return None


def build_program(dbg=False):
    nc = bass.Bass("TRN2", target_bir_lowering=False)
    es = ExitStack()

    def din(name, shape, dtype):
        return nc.dram_tensor(name, shape, dtype, kind="ExternalInput").ap()

    xseq = din("xseq", [S, H], F32)
    xq = din("xq", [1024, H], F32)
    wq = din("wq", [H, H], F32)
    wkr = din("wkr", [512, H], F32)
    wdk = din("wdk", [CD, H], F32)
    wupk = din("wupk", [512, CD], F32)
    wupv = din("wupv", [1024, CD], F32)
    wo = din("wo", [H, H], F32)
    gq = din("gq", [II, H], I32)
    gsi = din("gs", [II, 32], F32)
    uqi = din("uq", [II, H], I32)
    usi = din("us", [II, 32], F32)
    dqi = din("dq", [H, II], I32)
    dsi = din("ds", [H, 112], F32)
    consts_in = din("consts", [128, len(CONST_VALS)], F32)
    ident_in = din("ident", [128, 128], F32)
    cosq_in = din("cosq", [128, 1024], BF16)
    sinq_in = din("sinq", [128, 1024], BF16)
    cosk_in = din("cosk", [64, S], BF16)
    sink_in = din("sinkp", [64, S], BF16)
    masks_in = din("masks", [128, 32 * 512], BF16)
    esel_in = din("esel", [4, 512], F32)
    e32_in = din("e32", [128, H], BF16)
    zpadi = din("zpad", [II, 96], BF16)

    out_t = nc.dram_tensor("out", [1024, H], F32, kind="ExternalOutput").ap()
    if dbg:
        dbg_ckv = nc.dram_tensor("dbg_ckv", [128, 4 * S], BF16, kind="ExternalOutput").ap()
        dbg_kf = nc.dram_tensor("dbg_kf", [128, 8 * S], BF16, kind="ExternalOutput").ap()
        dbg_vb = nc.dram_tensor("dbg_vb", [128, 32 * 128], BF16, kind="ExternalOutput").ap()
        dbg_p0 = nc.dram_tensor("dbg_p0", [128, 512], BF16, kind="ExternalOutput").ap()
        dbg_l4 = nc.dram_tensor("dbg_l4", [128, 4], F32, kind="ExternalOutput").ap()
        dbg_rl = nc.dram_tensor("dbg_rl", [4, 128], F32, kind="ExternalOutput").ap()
        dbg_bc = nc.dram_tensor("dbg_bc", [128, 512], F32, kind="ExternalOutput").ap()
        dbg_w0 = nc.dram_tensor("dbg_w0", [128, 512], BF16, kind="ExternalOutput").ap()
        dbg_sg = nc.dram_tensor("dbg_sg", [128, 512], BF16, kind="ExternalOutput").ap()
        dbg_bcs = nc.dram_tensor("dbg_bcs", [128, 512], BF16, kind="ExternalOutput").ap()

    def dsc(name, shape, dtype):
        kind = "ExternalOutput" if dbg else "Internal"
        return nc.dram_tensor(name, shape, dtype, kind=kind).ap()

    xn_tok2 = dsc("xn_tok2", [32, S, 128], BF16)
    xnq_tok2 = dsc("xnq_tok2", [32, 1024, 128], BF16)
    wq_sc2 = dsc("wq_sc2", [32, H, 128], BF16)
    wo_sc2 = dsc("wo_sc2", [32, H, 128], BF16)
    wkr_sc2 = dsc("wkr_sc2", [32, 512, 128], BF16)
    wdk_sc2 = dsc("wdk_sc2", [32, CD, 128], BF16)
    wupk_sc2 = dsc("wupk_sc2", [4, 512, 128], BF16)
    wupv_sc2 = dsc("wupv_sc2", [4, 1024, 128], BF16)
    d_sc2 = dsc("d_sc2", [112, H, 128], BF16)
    gq_bf2 = dsc("gq_bf2", [32, II, 128], BF16)
    uq_bf2 = dsc("uq_bf2", [32, II, 128], BF16)
    gs_bf = dsc("gs_bf", [II, 128], BF16)
    us_bf = dsc("us_bf", [II, 128], BF16)
    dq_bf = dsc("dq_bf", [H, II], BF16)
    qsp = dsc("qsp", [H, 1024], BF16)
    asp = dsc("asp", [H, 1024], BF16)
    base = dsc("base", [1024, H], F32)
    inter = dsc("inter", [II, 1024], BF16)

    SB_BASE = 16512

    def sb_at(name, shape, dtype, off):
        return nc.alloc_sbuf_tensor_at(name, shape, dtype, offset=SB_BASE + off)

    A1 = 5984
    A2 = A1 + 65536
    A3 = A2 + 65536
    A4 = A3 + 32768
    A5 = A4 + 32768

    cc = sb_at("cc", [128, len(CONST_VALS)], F32, 0)
    ident = sb_at("ident", [128, 128], F32, 64)
    ones_bf = sb_at("ones_bf", [128, 1], BF16, 576)
    ones1 = sb_at("ones1", [1, 128], F32, 608)
    st_s1 = sb_at("st_s1", [128, 2], F32, 1120)
    st_s2 = sb_at("st_s2", [128, 2], F32, 1152)
    st_m2 = sb_at("st_m2", [128, 2], F32, 1184)
    st_var = sb_at("st_var", [128, 2], F32, 1216)
    st_rstd = sb_at("st_rstd", [128, 2], F32, 1248)
    st_nm = sb_at("st_nm", [128, 2], F32, 1280)
    st_nb = sb_at("st_nb", [128, 2], F32, 1312)
    t_l4 = sb_at("t_l4", [128, 4], F32, 1344)
    t_rl = sb_at("t_rl", [4, 128], F32, 1376)
    t_bc = sb_at("t_bc", [128, 512], F32, 1888)
    t_bc2 = sb_at("t_bc2", [128, 512], F32, 3936)
    esel = sb_at("esel", [4, 512], F32, 3936)

    ln_x = sb_at("ln_x", [128, 2 * H], F32, A1)
    ln_xn = sb_at("ln_xn", [128, 2 * H], BF16, A1 + 32768)
    ln_junk = sb_at("ln_junk", [128, H], BF16, A1 + 49152)
    wc_in = sb_at("wc_in", [128, 2 * H], F32, A2)
    wc_out = sb_at("wc_out", [128, 2 * H], BF16, A2 + 32768)
    xnfm = sb_at("xnfm", [128, 2 * 16 * 512], BF16, A1)
    wkrT = sb_at("wkrT", [128, 32 * 512], BF16, A1 + 32768)
    kfull = sb_at("kfull", [128, 8 * S], BF16, A2)
    ckv = sb_at("ckv", [128, 4 * S], BF16, A3)
    wdkT = sb_at("wdkT", [128, 32 * 512], BF16, A4)
    upkT = sb_at("upkT", [128, 4 * 512], BF16, A4)
    ropek = sb_at("ropek", [64, 4 * 512], BF16, A5)
    rsc = sb_at("rsc", [64, 2 * 512], F32, A5 + 4096)
    xnqfm = sb_at("xnqfm", [128, 32 * 1024], BF16, A1)
    wqT = sb_at("wqT", [128, 32 * 512], BF16, A4)
    cosq = sb_at("cosq", [128, 1024], BF16, A5)
    sinq = sb_at("sinq", [128, 1024], BF16, A5 + 2048)
    qstage = sb_at("qstage", [128, 2 * 512], BF16, A5 + 4096)
    masks = sb_at("masks", [128, 32 * 512], BF16, A4)
    vbuf = sb_at("vbuf", [128, 32 * 128], BF16, A1)
    qring = sb_at("qring", [128, 2 * 1024], BF16, A1 + 8192)
    pring = sb_at("pring", [128, 3 * 512], BF16, A1 + 12288)
    aout = sb_at("aout", [128, 2 * 512], BF16, A1 + 15360)
    upvT = sb_at("upvT", [128, 4 * 1024], BF16, A1 + 17408)
    abuf = sb_at("abuf", [128, 32 * 1024], BF16, A1)
    woT = sb_at("woT", [128, 2 * 32 * 512], BF16, A2)
    ph5x = sb_at("ph5x", [128, 2 * 512], F32, A5)
    ph5y = sb_at("ph5y", [128, 2 * 512], F32, A5 + 4096)
    xnqfm2 = sb_at("xnqfm2", [128, 32 * 1024], BF16, A1)
    wspan = sb_at("wspan", [128, 2 * 32 * 512], BF16, A2)
    e32 = sb_at("e32", [128, H], BF16, A3)
    sgts = sb_at("sgts", [128, 2 * 512], BF16, A3 + 8192)
    bc_sb = sb_at("bc_sb", [128, 2 * 512], BF16, A3 + 12288)
    dqraw = sb_at("dqraw", [128, 2 * H], BF16, A4)
    dqout = sb_at("dqout", [128, 2 * H], BF16, A4 + 16384)
    gsbuf = sb_at("gsbuf", [128, 8 * 512], BF16, A5)
    ffev = sb_at("ffev", [128, 4 * 512], BF16, 1888)
    acc = sb_at("acc", [128, 8 * H], F32, A1)
    wdT03 = sb_at("wdT03", [128, 4 * H], BF16, A3)
    i16 = sb_at("i16", [128, 7 * 1024], BF16, A4)
    wdT45 = sb_at("wdT45", [128, 2 * H], BF16, A4 + 14336)
    wdT6 = sb_at("wdT6", [128, H], BF16, A5)
    dsc_sb = sb_at("dsc_sb", [128, 3584], F32, A3 + 14336)
    ph7x = sb_at("ph7x", [128, 2 * 512], F32, 1888)

    for i, v in enumerate(CONST_VALS):
        nc.const_aps.register(F32, float(v), cc[:, i:i + 1])

    pb = [es.enter_context(nc.psum_tensor(f"pb{i}", [128, 512], F32)) for i in range(8)]

    pl = Plan(nc, es)

    pl.load(cc[:], consts_in)
    pl.load(ident[:], ident_in)
    pl.dve(lambda e: e.memset(ones_bf[:], 1.0))
    pl.dve(lambda e: e.memset(ones1[:], 1.0), inc=True)

    wc_tot = {}
    def cast_chunked(srcw, dstw, nrow, chn):
        step = 256 if nrow >= 256 else nrow
        for r0 in range(0, nrow, step):
            wc_tot[chn] = pl.load(dstw[:, r0:r0 + step, :],
                                  srcw[r0:r0 + step, :].rearrange("r (c k) -> c r k", k=128),
                                  queue="gpsimd", ch=chn)
    def cast_flat(srcw, dstw, nrow, chn):
        for r0 in range(0, nrow, 2048):
            wc_tot[chn] = pl.load(dstw[r0:r0 + 2048, :], srcw[r0:r0 + 2048, :],
                                  queue="gpsimd", ch=chn)
    kr_cast_jobs = [(wkr, wkr_sc2, 512), (wdk, wdk_sc2, CD),
                    (wupk, wupk_sc2, 512), (wupv, wupv_sc2, 1024)]
    def cast_chunked448(srcw, dstw, chn):
        for r0 in range(0, II, 448):
            wc_tot[chn] = pl.load(dstw[:, r0:r0 + 448, :],
                                  srcw[r0:r0 + 448, :].rearrange("r (c k) -> c r k", k=128),
                                  queue="gpsimd", ch=chn)

    ln_jobs = [(xseq[t * 128:(t + 1) * 128, :],
                xn_tok2[:, t * 128:(t + 1) * 128, :].rearrange("c t k -> t c k")) for t in range(32)]
    ln_jobs += [(xq[t * 128:(t + 1) * 128, :],
                 xnq_tok2[:, t * 128:(t + 1) * 128, :].rearrange("c t k -> t c k")) for t in range(8)]

    ln_st_v = [0, 0]
    ln_act_v = [0, 0]
    ln_tile_st = {}
    kr_emitted = [False]
    for j, (src, dst) in enumerate(ln_jobs):
        if j == 20 and not kr_emitted[0]:
            kr_emitted[0] = True
            pl.wait("gpsimd", "eact", pl.n("eact"))
            for a_, b_, n_ in kr_cast_jobs:
                cast_chunked(a_, b_, n_, "wc_kr")
        slot = j % 2
        xt = ln_x[:, slot * H:(slot + 1) * H]
        xo = ln_xn[:, slot * H:(slot + 1) * H]
        pl.wait("sync", f"lns{slot}", ln_st_v[slot])
        pl.wait("sync", "eact", ln_act_v[slot])
        lv = pl.load(xt, src, ch=f"lnx{slot}")
        s1 = st_s1[:, slot:slot + 1]
        s2 = st_s2[:, slot:slot + 1]
        m2 = st_m2[:, slot:slot + 1]
        var = st_var[:, slot:slot + 1]
        rstd = st_rstd[:, slot:slot + 1]
        nm = st_nm[:, slot:slot + 1]
        nb = st_nb[:, slot:slot + 1]
        pl.wait("vector", f"lnx{slot}", lv)
        dv1 = pl.dve(lambda e, xt=xt, s1=s1: e.tensor_scalar(
            out=ln_junk[:], in0=xt, scalar1=1.0, scalar2=None,
            op0=ALU.mult, op1=ALU.add, accum_out=s1), inc=True)
        pl.wait("scalar", f"lnx{slot}", lv)
        sq = pl.act(lambda e, xt=xt, s2=s2, xo=xo:
                    e.activation(out=xo, in_=xt, func=AF.Square, accum_out=s2), inc=True)
        pl.wait("vector", "edve", dv1)
        pl.wait("vector", "eact", sq)
        dv = pl.dve(lambda e, s1=s1, s2=s2, m2=m2, var=var, nm=nm:
                    (e.scalar_tensor_tensor(out=m2, in0=s1, scalar=1.0 / (H * float(H)),
                                            in1=s1, op0=ALU.mult, op1=ALU.mult),
                     e.tensor_scalar(out=var, in0=s2, scalar1=1.0 / H, scalar2=None, op0=ALU.mult),
                     e.tensor_scalar(out=nm, in0=s1, scalar1=-1.0 / H, scalar2=None, op0=ALU.mult))[2],
                    inc=True)
        pl.wait("vector", "edve", dv)
        dv = pl.dve(lambda e, var=var, m2=m2: e.tensor_sub(var, var, m2), inc=True)
        pl.wait("scalar", "edve", dv)
        av = pl.act(lambda e, var=var, rstd=rstd:
                    (e.activation(out=rstd, in_=var, func=AF.Ln, bias=1e-5),
                     e.activation(out=rstd, in_=rstd, func=AF.Exp, scale=-0.5))[1], inc=True)
        pl.wait("vector", "eact", av)
        dv = pl.dve(lambda e, nb=nb, nm=nm, rstd=rstd: e.tensor_mul(nb, nm, rstd), inc=True)
        pl.wait("scalar", "edve", dv)
        pl.wait("scalar", "eact", av)
        pl.wait("scalar", f"lnx{slot}", lv)
        av = pl.act(lambda e, xt=xt, xo=xo, rstd=rstd, nb=nb:
                    e.activation(out=xo, in_=xt, func=AF.Identity, scale=rstd, bias=nb), inc=True)
        ln_act_v[slot] = av
        pl.wait("scalar", "eact", av)
        ln_st_v[slot] = pl.store(dst, xo.rearrange("p (c k) -> p c k", k=128), ch=f"lns{slot}")
        ln_tile_st[j] = (f"lns{slot}", ln_st_v[slot])
    ln_done_v = pl.n("eact")
    pl.wait("gpsimd", "eact", max(0, ln_done_v - 6))
    cast_chunked(wq, wq_sc2, H, "wc_q")
    wc_tot["wc_d"] = pl.load(dq_bf[0:2048, :], dqi[0:2048, :], queue="gpsimd", ch="wc_d")
    wc_tot["wc_g"] = pl.load(gs_bf[:, 0:32], gsi[:], queue="gpsimd", ch="wc_g")
    wc_tot["wc_g"] = pl.load(gs_bf[:, 32:128], zpadi[:], queue="gpsimd", ch="wc_g")
    cast_chunked448(gq, gq_bf2, "wc_g")
    cast_chunked(wo, wo_sc2, H, "wc_o")
    wc_tot["wc_u"] = pl.load(us_bf[:, 0:32], usi[:], queue="gpsimd", ch="wc_u")
    wc_tot["wc_u"] = pl.load(us_bf[:, 32:128], zpadi[:], queue="gpsimd", ch="wc_u")
    cast_chunked448(uqi, uq_bf2, "wc_u")

    pl.wait("sync", "wc_kr", wc_tot["wc_kr"])
    pl.wait("sync", "eact", ln_done_v)
    wtv = 0
    for hc in range(NHC):
        wtv = pl.load(wdkT[:, hc * 512:(hc + 1) * 512], wdk_sc2[hc, :, :], transpose=True, ch="wt")
        wtv = pl.load(wkrT[:, hc * 512:(hc + 1) * 512], wkr_sc2[hc, :, :], transpose=True, ch="wt")
    pl.wait("tensor", "wt", wtv)

    half_pe = [0, 0]
    rope_done_sb = [0] * 8
    rope_done_sb2 = [0] * 8
    rope_scratch_free = [0]
    bank_free = {}
    kc_bank_free = {}
    for sb in range(8):
        slot2 = sb % 2
        hl_v = [0, 0]
        for h2 in range(2):
            buf = h2
            pl.wait("sync", "epe", half_pe[buf])
            for t in range(sb * 4, sb * 4 + 4):
                chn, v = ln_tile_st[t]
                pl.wait("sync", chn, v)
            lastv = 0
            for hc in range(h2 * 16, h2 * 16 + 16):
                lastv = pl.load(xnfm[:, buf * 8192 + (hc - h2 * 16) * 512: buf * 8192 + (hc - h2 * 16 + 1) * 512],
                                xn_tok2[hc, sb * 512:(sb + 1) * 512, :],
                                transpose=True, ch=f"xa{buf}")
            hl_v[h2] = lastv
        if sb >= 2:
            pl.wait("sync", "eact", rope_done_sb[sb - 2])
            pl.wait("sync", "edve", rope_done_sb2[sb - 2])
        pl.load(ropek[:, slot2 * 1024: slot2 * 1024 + 512], cosk_in[:, sb * 512:(sb + 1) * 512], ch=f"xa{0}")
        rv = pl.load(ropek[:, slot2 * 1024 + 512: slot2 * 1024 + 1024], sink_in[:, sb * 512:(sb + 1) * 512], ch=f"xa{0}")
        cosr = ropek[0:64, slot2 * 1024: slot2 * 1024 + 512]
        sinr = ropek[0:64, slot2 * 1024 + 512: slot2 * 1024 + 1024]
        pv_grp = [None] * 8
        for h2 in range(2):
            buf = h2
            pl.wait("tensor", f"xa{buf}", hl_v[h2])
            for grp in range(8):
                bank = pb[grp]
                if h2 == 0 and grp in bank_free:
                    pl.wait("tensor", bank_free[grp][0], bank_free[grp][1])
                wT = wdkT if grp < 4 else wkrT
                oc = grp % 4
                for hcl in range(16):
                    hc = h2 * 16 + hcl
                    lhsT = wT[:, hc * 512 + oc * 128: hc * 512 + (oc + 1) * 128]
                    rhs = xnfm[:, buf * 8192 + hcl * 512: buf * 8192 + (hcl + 1) * 512]
                    pv_grp[grp] = pl.mm(bank[:], lhsT, rhs, hc == 0, hc == NHC - 1,
                                        inc=(hc == NHC - 1))
            half_pe[buf] = pl.n("epe")
        for grp in range(4):
            pl.wait("scalar", "epe", pv_grp[grp])
            av = pl.act(lambda e, o=ckv[:, grp * S + sb * 512: grp * S + (sb + 1) * 512], i=pb[grp][:]:
                        e.activation(out=o, in_=i, func=AF.Identity), inc=True)
            bank_free[grp] = ("eact", av)
        for kc in range(4):
            grp = 4 + kc
            bank = pb[grp]
            pl.wait("scalar", "epe", pv_grp[grp])
            pl.wait("vector", f"xa{0}", rv)
            dvl = None
            avl = None
            for hh in range(2):
                g = 2 * kc + hh
                b0 = 64 * hh
                pl.wait("scalar", "edve", rope_scratch_free[0])
                avl = pl.act(lambda e, i=bank, b0=b0:
                             (e.activation(out=t_bc[0:64, :], in_=i[b0:b0 + 64, :], func=AF.Identity),
                              e.activation(out=t_bc2[0:32, :], in_=i[b0 + 32:b0 + 64, :], func=AF.Identity),
                              e.activation(out=t_bc2[32:64, :], in_=i[b0:b0 + 32, :], func=AF.Identity))[2],
                             inc=True)
                pl.wait("vector", "eact", avl)
                dv0 = pl.dve(lambda e, c=cosr, s=sinr:
                             (e.tensor_tensor(out=rsc[0:64, 0:512], in0=t_bc[0:64, :], in1=c, op=ALU.mult),
                              e.tensor_tensor(out=rsc[0:64, 512:1024], in0=t_bc2[0:64, :], in1=s, op=ALU.mult))[1],
                             inc=True)
                pl.wait("vector", "edve", dv0)
                dv1 = pl.dve(lambda e:
                             e.tensor_tensor(out=t_bc[0:64, :], in0=rsc[0:64, 0:512],
                                             in1=rsc[0:64, 512:1024], op=ALU.add), inc=True)
                pl.wait("vector", "edve", dv1)
                klo = kfull[0:32, g * S + sb * 512: g * S + sb * 512 + 512]
                khi = kfull[64:96, g * S + sb * 512: g * S + sb * 512 + 512]
                dvl = pl.dve(lambda e, klo=klo, khi=khi:
                             (e.tensor_copy(klo, t_bc[0:32, :]),
                              e.tensor_copy(khi, t_bc[32:64, :]))[1], inc=True)
                rope_scratch_free[0] = dvl
            bank_free[grp] = ("eact", avl)
        rope_done_sb[sb] = pl.n("eact")
        rope_done_sb2[sb] = pl.n("edve")

    pl.barrier()

    upv = 0
    for ccc in range(4):
        upv = pl.load(upkT[:, ccc * 512:(ccc + 1) * 512], wupk_sc2[ccc, :, :], transpose=True, ch="wt")
    pl.wait("tensor", "wt", upv)
    kc_bank_free = {}
    for g in range(KVH):
        for sb in range(8):
            bk = (g * 8 + sb) % 2
            bank = pb[bk]
            if bk in kc_bank_free:
                pl.wait("tensor", "eact", kc_bank_free[bk])
            pv = None
            for ccc in range(4):
                lhsT = upkT[:, ccc * 512 + g * 64: ccc * 512 + (g + 1) * 64]
                rhs = ckv[:, ccc * S + sb * 512: ccc * S + (sb + 1) * 512]
                pv = pl.mm(bank[0:64, :], lhsT, rhs, ccc == 0, ccc == 3, inc=(ccc == 3))
            pl.wait("scalar", "epe", pv)
            k1 = kfull[32:64, g * S + sb * 512: g * S + sb * 512 + 512]
            k2 = kfull[96:128, g * S + sb * 512: g * S + sb * 512 + 512]
            av = pl.act(lambda e, k1=k1, k2=k2, i=bank:
                        (e.activation(out=k1, in_=i[0:32, :], func=AF.Identity),
                         e.activation(out=k2, in_=i[32:64, :], func=AF.Identity))[1], inc=True)
            kc_bank_free[bk] = av

    pl.barrier()

    for hc in range(NHC):
        pl.load(xnqfm[:, hc * 1024:(hc + 1) * 1024], xnq_tok2[hc, :, :], transpose=True)
    pl.load(cosq[:], cosq_in)
    pl.load(sinq[:], sinq_in)
    pl.barrier()

    q_st_v = [0, 0]
    wq_used = 0
    q_scratch_free = [0]
    bank_free = {}
    for hg in range(8):
        pl.wait("sync", "epe", wq_used)
        if hg == 0:
            pl.wait("sync", "wc_q", wc_tot["wc_q"])
        lastv = 0
        for hc in range(NHC):
            lastv = pl.load(wqT[:, hc * 512:(hc + 1) * 512],
                            wq_sc2[hc, hg * 512:(hg + 1) * 512, :],
                            transpose=True, ch="wqt")
        pl.wait("tensor", "wqt", lastv)
        for hl in range(4):
            head = hg * 4 + hl
            for tb in range(2):
                bk = (head * 2 + tb) % 2
                bank = pb[bk]
                if bk in bank_free:
                    pl.wait("tensor", bank_free[bk][0], bank_free[bk][1])
                pv = None
                for hc in range(NHC):
                    lhsT = wqT[:, hc * 512 + hl * 128: hc * 512 + (hl + 1) * 128]
                    rhs = xnqfm[:, hc * 1024 + tb * 512: hc * 1024 + (tb + 1) * 512]
                    pv = pl.mm(bank[:], lhsT, rhs, hc == 0, hc == NHC - 1, inc=(hc == NHC - 1))
                qslot = (head * 2 + tb) % 2
                pl.wait("scalar", "epe", pv)
                pl.wait("scalar", "edve", q_scratch_free[0])
                av = pl.act(lambda e, i=bank:
                            (e.activation(out=t_bc[:], in_=i[:], func=AF.Identity),
                             e.activation(out=t_bc2[0:64, :], in_=i[64:128, :], func=AF.Identity),
                             e.activation(out=t_bc2[64:128, :], in_=i[0:64, :], func=AF.Identity))[2],
                            inc=True)
                pl.wait("vector", f"qs{qslot}", q_st_v[qslot])
                pl.wait("vector", "eact", av)
                ctab = cosq[:, tb * 512:(tb + 1) * 512]
                stab = sinq[:, tb * 512:(tb + 1) * 512]
                qs = qstage[:, qslot * 512:(qslot + 1) * 512]
                dv0 = pl.dve(lambda e, c=ctab, s=stab:
                             (e.tensor_tensor(out=t_bc[:], in0=t_bc[:], in1=c, op=ALU.mult),
                              e.tensor_tensor(out=t_bc2[:], in0=t_bc2[:], in1=s, op=ALU.mult))[1], inc=True)
                pl.wait("vector", "edve", dv0)
                dv1 = pl.dve(lambda e, qs=qs:
                             e.tensor_tensor(out=qs, in0=t_bc[:], in1=t_bc2[:], op=ALU.add), inc=True)
                q_scratch_free[0] = dv1
                bank_free[bk] = ("eact", av)
                pl.wait("scalar", "edve", dv1)
                q_st_v[qslot] = pl.store(qsp[head * 128:(head + 1) * 128, tb * 512:(tb + 1) * 512], qs,
                                         ch=f"qs{qslot}")
        wq_used = pl.n("epe")

    pl.barrier()

    pl.load(masks[:], masks_in)
    pl.load(esel[:], esel_in)
    pl.wait("gpsimd", "eact", pl.n("eact"))
    wc_tot["wc_d"] = pl.load(dq_bf[2048:4096, :], dqi[2048:4096, :], queue="gpsimd", ch="wc_d")
    for ccc in range(4):
        pl.load(upvT[:, ccc * 1024:(ccc + 1) * 1024], wupv_sc2[ccc, :, :], transpose=True)
    pl.barrier()

    ast_v = [0, 0]
    q_used = [0, 0]
    prev_exp_last = 0
    for g in range(KVH):
        pl.wait("tensor", "edve", pl.n("edve"))
        vb_free = {}
        for sc in range(32):
            bk = 4 + sc % 2
            bank = pb[bk]
            if bk in vb_free:
                pl.wait("tensor", "eact", vb_free[bk])
            pv = None
            for ccc in range(4):
                lhsT = ckv[:, ccc * S + sc * 128: ccc * S + (sc + 1) * 128]
                rhs = upvT[:, ccc * 1024 + g * 128: ccc * 1024 + (g + 1) * 128]
                pv = pl.mm(bank[:, 0:128], lhsT, rhs, ccc == 0, ccc == 3, inc=(ccc == 3))
            pl.wait("scalar", "epe", pv)
            av = pl.act(lambda e, o=vbuf[:, sc * 128:(sc + 1) * 128], i=bank[:, 0:128]:
                        e.activation(out=o, in_=i, func=AF.Identity), inc=True)
            vb_free[bk] = av
        vready = pl.n("eact")
        if dbg and g == 0:
            pl.wait("scalar", "eact", vready)
            pl.store(dbg_vb, vbuf[:], ch="dbgs")

        for hl in range(4):
            head = g * 4 + hl
            qslot = head % 2
            pl.wait("sync", "epe", q_used[qslot])
            qlv = pl.load(qring[:, qslot * 1024:(qslot + 1) * 1024], qsp[head * 128:(head + 1) * 128, :],
                          ch=f"qr{qslot}")
            for tb in range(2):
                T = TRIPS[tb]
                par = (head * 2 + tb) % 2
                obank = pb[2] if par == 0 else pb[6]
                lbank = pb[3] if par == 0 else pb[7]
                pl.wait("tensor", f"qr{qslot}", qlv)
                pl.wait("tensor", "eact", vready)
                pl.wait("tensor", "edve", pl.n("edve"))
                exp_v = {}
                p_ready = {}
                av_done = {}
                o_done = None
                l_done = None
                for k in range(T + 1):
                    if k < T:
                        if k >= 2:
                            pl.wait("tensor", "eact", exp_v[k - 2])
                        else:
                            pl.wait("tensor", "eact", prev_exp_last)
                        lhsT = kfull[:, g * S + k * 128: g * S + (k + 1) * 128]
                        rhs = qring[:, qslot * 1024 + tb * 512: qslot * 1024 + (tb + 1) * 512]
                        sv = pl.mm(pb[k % 2][:], lhsT, rhs, True, True, inc=True)
                        pl.wait("scalar", "epe", sv)
                        if k >= 3:
                            pl.wait("scalar", "epe", av_done[k - 3])
                        ptile = pring[:, (k % 3) * 512:((k % 3) + 1) * 512]
                        av = pl.act(lambda e, o=ptile, i=pb[k % 2][:]:
                                    e.activation(out=o, in_=i, func=AF.Exp, scale=SM_SCALE,
                                                 bias=EXP_SHIFT), inc=True)
                        exp_v[k] = av
                        if dbg and g == 0 and hl == 0 and tb == 0 and k == 0:
                            pl.wait("scalar", "eact", av)
                            pl.store(dbg_p0, ptile, ch="dbgs")
                        masked = (tb == 1) or (k >= 16)
                        if masked:
                            mk = (16 + k) if tb == 1 else (k - 16)
                            pl.wait("vector", "eact", av)
                            dv = pl.dve(lambda e, o=ptile, m=masks[:, mk * 512:(mk + 1) * 512]:
                                        e.tensor_tensor(out=o, in0=o, in1=m, op=ALU.mult), inc=True)
                            p_ready[k] = ("edve", dv)
                        else:
                            p_ready[k] = ("eact", av)
                    kk = k - 1
                    if kk >= 0:
                        semn, pval = p_ready[kk]
                        pl.wait("tensor", semn, pval)
                        ptile = pring[:, (kk % 3) * 512:((kk % 3) + 1) * 512]
                        ov = pl.mm(obank[:], vbuf[:, kk * 128:(kk + 1) * 128], ptile,
                                   kk == 0, kk == T - 1, inc=(kk == T - 1))
                        if kk == T - 1:
                            o_done = ov
                        for sub in range(4):
                            lv_ = pl.mm(lbank[:, sub:sub + 1], ptile[:, sub * 128:(sub + 1) * 128],
                                        ones_bf[:], kk == 0 and sub == 0,
                                        kk == T - 1 and sub == 3,
                                        inc=(kk == T - 1 and sub == 3))
                            if kk == T - 1 and sub == 3:
                                l_done = lv_
                        av_done[kk] = pl.n("epe")
                prev_exp_last = exp_v[T - 1]
                pl.wait("vector", "epe", l_done)
                dv = pl.dve(lambda e, i=lbank[:, 0:4]: e.tensor_copy(t_l4[:], i), inc=True)
                if dbg and g == 0 and hl == 0 and tb == 0:
                    pl.wait("scalar", "edve", dv)
                    pl.store(dbg_l4, t_l4[:], ch="dbgs")
                pl.wait("tensor", "edve", dv)
                tv = pl.bump("epe", 1)
                pl.emit("tensor", lambda e, h=pl.h("epe"):
                        e.transpose(pb[4][0:4, 0:128], t_l4[:], ident[:]).then_inc(h, 1))
                pl.wait("vector", "epe", tv)
                dv = pl.dve(lambda e: e.reciprocal(t_rl[:], pb[4][0:4, 0:128]), inc=True)
                pl.wait("tensor", "edve", dv)
                bcv = None
                for sub in range(4):
                    bcv = pl.mm(pb[5][:, sub * 128:(sub + 1) * 128],
                                esel[:, sub * 128:(sub + 1) * 128], t_rl[0:4, 0:128],
                                sub == 0, sub == 3, inc=(sub == 3))
                if dbg and g == 0 and hl == 0 and tb == 0:
                    pl.wait("scalar", "edve", pl.n("edve"))
                    pl.store(dbg_rl, t_rl[:], ch="dbgs")
                pl.wait("scalar", "epe", bcv)
                av = pl.act(lambda e: e.activation(out=t_bc[:], in_=pb[5][:], func=AF.Identity), inc=True)
                aslot = par
                pl.wait("vector", f"as{aslot}", ast_v[aslot])
                pl.wait("vector", "eact", av)
                pl.wait("vector", "epe", o_done)
                if dbg and g == 0 and hl == 0 and tb == 0:
                    pl.wait("scalar", "eact", av)
                    pl.store(dbg_bc, t_bc[:], ch="dbgs")
                at = aout[:, aslot * 512:(aslot + 1) * 512]
                dv = pl.dve(lambda e, at=at, ob=obank[:]:
                            e.tensor_tensor(out=at, in0=ob, in1=t_bc[:], op=ALU.mult), inc=True)
                pl.wait("scalar", "edve", dv)
                ast_v[aslot] = pl.store(asp[head * 128:(head + 1) * 128, tb * 512:(tb + 1) * 512], at,
                                        ch=f"as{aslot}")
            q_used[qslot] = pl.n("epe")

    pl.barrier()


    dq_jobs = []
    for hc in range(32):
        for qtr in range(8):
            dq_jobs.append(("d", hc, qtr))
    dq_done_v = {}
    dq_state = [0]
    dq_dsl = [None]
    dq_raw_free = [None] * 4
    dq_out_free = [0] * 4

    def emit_dequant(njobs):
        for _ in range(njobs):
            if dq_state[0] >= len(dq_jobs):
                return
            which, ic, hf = dq_jobs[dq_state[0]]
            dq_state[0] += 1
            slot = dq_state[0] % 4
            if dq_dsl[0] is None:
                dq_dsl[0] = pl.load(dsc_sb[:].rearrange("p (n g) -> p n g", g=112),
                                    dsi.rearrange("(n p) g -> p n g", p=128), ch="sc")
            scl_ap = dsc_sb[:, ic * 112 + hf * 14: ic * 112 + (hf + 1) * 14]
            ng = 14
            raw = dqraw[:, slot * 2048: slot * 2048 + 1792]
            outb = dqout[:, slot * 2048: slot * 2048 + 1792]
            srcap = dq_bf[ic * 128:(ic + 1) * 128, hf * 1792:(hf + 1) * 1792]
            dst = d_sc2[hf * 14:(hf + 1) * 14, ic * 128:(ic + 1) * 128, :].rearrange("c h k -> h c k")
            if dq_raw_free[slot]:
                pl.wait("sync", dq_raw_free[slot][0], dq_raw_free[slot][1])
            pl.wait("sync", "wc_" + which, wc_tot["wc_" + which])
            gv = pl.load(raw, srcap, ch=f"dr{slot}")
            scv = dq_dsl[0]
            pl.wait("vector", f"dr{slot}", gv)
            pl.wait("vector", "sc", scv)
            pl.wait("vector", f"dqs{slot}", dq_out_free[slot])
            dv = pl.dve(lambda e, o=outb, i=raw, s=scl_ap, ng=ng:
                        e.tensor_tensor(out=o.rearrange("p (g k) -> p g k", k=128),
                                        in0=i.rearrange("p (g k) -> p g k", k=128),
                                        in1=s.to_broadcast([128, ng, 128]),
                                        op=ALU.mult), inc=True)
            dq_raw_free[slot] = ("edve", dv)
            pl.wait("scalar", "edve", dv)
            stv = pl.store(dst, outb.rearrange("p (c k) -> p c k", k=128), ch=f"dqs{slot}")
            dq_out_free[slot] = stv

    e32_lv = pl.load(e32[:], e32_in, ch="sc")
    emit_dequant(8)
    abv = 0
    for dc in range(NHC):
        abv = pl.load(abuf[:, dc * 1024:(dc + 1) * 1024], asp[dc * 128:(dc + 1) * 128, :], ch="ab")
    pl.wait("tensor", "ab", abv)
    wo_used = [0, 0]
    bank_free = {}
    y_dve = [0, 0]
    y_st = [0, 0]
    for hs in range(8):
        slot = hs % 2
        if hs == 0:
            pl.wait("sync", "wc_o", wc_tot["wc_o"])
        pl.wait("sync", "epe", wo_used[slot])
        lastv = 0
        for dc in range(NHC):
            lastv = pl.load(woT[:, slot * 16384 + dc * 512: slot * 16384 + (dc + 1) * 512],
                            wo_sc2[dc, hs * 512:(hs + 1) * 512, :],
                            transpose=True, ch=f"wo{slot}")
        pl.wait("tensor", f"wo{slot}", lastv)
        for tc in range(8):
            bk = tc % 2
            bank = pb[bk]
            if bk in bank_free:
                pl.wait("tensor", "edve", bank_free[bk])
            pv = None
            for dc in range(NHC):
                lhsT = abuf[:, dc * 1024 + tc * 128: dc * 1024 + (tc + 1) * 128]
                rhs = woT[:, slot * 16384 + dc * 512: slot * 16384 + (dc + 1) * 512]
                pv = pl.mm(bank[:], lhsT, rhs, dc == 0, dc == NHC - 1, inc=(dc == NHC - 1))
            emit_dequant(2)
            xslot = tc % 2
            pl.wait("sync", "edve", y_dve[xslot])
            xv = pl.load(ph5x[:, xslot * 512:(xslot + 1) * 512],
                         xq[tc * 128:(tc + 1) * 128, hs * 512:(hs + 1) * 512], ch=f"px{xslot}")
            pl.wait("vector", f"px{xslot}", xv)
            pl.wait("vector", "epe", pv)
            pl.wait("vector", f"bs{xslot}", y_st[xslot])
            yt = ph5y[:, xslot * 512:(xslot + 1) * 512]
            dv = pl.dve(lambda e, yt=yt, a=bank[:], b=ph5x[:, xslot * 512:(xslot + 1) * 512]:
                        e.tensor_tensor(out=yt, in0=a, in1=b, op=ALU.add), inc=True)
            y_dve[xslot] = dv
            bank_free[bk] = dv
            pl.wait("scalar", "edve", dv)
            y_st[xslot] = pl.store(base[tc * 128:(tc + 1) * 128, hs * 512:(hs + 1) * 512], yt,
                                   ch=f"bs{xslot}")
        wo_used[slot] = pl.n("epe")

    pl.barrier()

    xnq2_lv = 0
    for hc in range(NHC):
        xnq2_lv = pl.load(xnqfm2[:, hc * 1024:(hc + 1) * 1024], xnq_tok2[hc, :, :],
                          transpose=True, ch="xn2")
    span_used = [0, 0]
    gs_act = {}
    ff_st = [0, 0, 0, 0]
    gbank_free = {}
    ubank_free = {}
    sg_free = [0, 0]
    bc_free = {}
    bcsb_free = [0, 0]
    for sp_ in range(28):
        for which in ("g", "u"):
            slot = (sp_ * 2 + (0 if which == "g" else 1)) % 2
            emit_dequant(5)
            if sp_ == 0:
                pl.wait("sync", "wc_" + which, wc_tot["wc_" + which])
            pl.wait("sync", "epe", span_used[slot])
            src_sc = gq_bf2 if which == "g" else uq_bf2
            src_scale = gs_bf if which == "g" else us_bf
            sslot = slot
            pl.wait("sync", "epe", sg_free[sslot])
            sgv = pl.load(sgts[:, sslot * 512:(sslot + 1) * 512],
                          src_scale[sp_ * 512:(sp_ + 1) * 512, :], transpose=True, ch=f"sg{sslot}")
            ld_v = []
            for hc in range(NHC):
                ld_v.append(pl.load(wspan[:, slot * 16384 + hc * 512: slot * 16384 + (hc + 1) * 512],
                                    src_sc[hc, sp_ * 512:(sp_ + 1) * 512, :],
                                    transpose=True, ch=f"ws{slot}"))
            tt_last = 0
            for hc in range(NHC):
                bk = 6 + hc % 2
                if bk in bc_free:
                    pl.wait("tensor", "eact", bc_free[bk])
                pl.wait("tensor", f"sg{sslot}", sgv)
                if sp_ == 0 and hc == 0:
                    pl.wait("tensor", "sc", e32_lv)
                bmv = pl.mm(pb[bk][:], e32[:, hc * 128:(hc + 1) * 128],
                            sgts[:, sslot * 512:(sslot + 1) * 512], True, True, inc=True)
                pl.wait("scalar", "epe", bmv)
                pl.wait("scalar", "edve", bcsb_free[hc % 2])
                av = pl.act(lambda e, o=bc_sb[:, (hc % 2) * 512:((hc % 2) + 1) * 512], i=pb[bk][:]:
                            e.activation(out=o, in_=i, func=AF.Identity), inc=True)
                bc_free[bk] = av
                wtile = wspan[:, slot * 16384 + hc * 512: slot * 16384 + (hc + 1) * 512]
                pl.wait("vector", f"ws{slot}", ld_v[hc])
                pl.wait("vector", "eact", av)
                tt_last = pl.dve(lambda e, w=wtile, b=bc_sb[:, (hc % 2) * 512:((hc % 2) + 1) * 512]:
                                 e.tensor_tensor(out=w, in0=w, in1=b, op=ALU.mult), inc=True)
                bcsb_free[hc % 2] = tt_last
            sg_free[sslot] = pl.n("epe")
            if dbg and sp_ == 0 and which == "g":
                pl.wait("scalar", "edve", tt_last)
                pl.store(dbg_w0, wspan[:, 0:512], ch="dbgs")
                pl.store(dbg_sg, sgts[:, 0:512], ch="dbgs")
                pl.store(dbg_bcs, bc_sb[:, 512:1024], ch="dbgs")
            pl.wait("tensor", "edve", tt_last)
            pl.wait("tensor", "xn2", xnq2_lv)
            for isub in range(4):
                for tb in range(2):
                    pi = isub * 2 + tb
                    if which == "g":
                        bk = pi % 4
                        bank = pb[bk]
                        if bk in gbank_free:
                            pl.wait("tensor", "eact", gbank_free[bk])
                    else:
                        bk = 4 + pi % 2
                        bank = pb[bk]
                        if bk in ubank_free:
                            pl.wait("tensor", "edve", ubank_free[bk])
                    pv = None
                    for hc in range(NHC):
                        lhsT = wspan[:, slot * 16384 + hc * 512 + isub * 128:
                                     slot * 16384 + hc * 512 + (isub + 1) * 128]
                        rhs = xnqfm2[:, hc * 1024 + tb * 512: hc * 1024 + (tb + 1) * 512]
                        pv = pl.mm(bank[:], lhsT, rhs, hc == 0, hc == NHC - 1, inc=(hc == NHC - 1))
                    if which == "g":
                        pl.wait("scalar", "epe", pv)
                        av = pl.act(lambda e, o=gsbuf[:, pi * 512:(pi + 1) * 512], i=bank[:]:
                                    e.activation(out=o, in_=i, func=AF.Silu), inc=True)
                        gs_act[(sp_, pi)] = av
                        gbank_free[bk] = av
                    else:
                        fslot = pi % 4
                        pl.wait("vector", f"fs{fslot}", ff_st[fslot])
                        pl.wait("vector", "epe", pv)
                        pl.wait("vector", "eact", gs_act[(sp_, pi)])
                        ft = ffev[:, fslot * 512:(fslot + 1) * 512]
                        dv = pl.dve(lambda e, ft=ft, a=bank[:], b=gsbuf[:, pi * 512:(pi + 1) * 512]:
                                    e.tensor_tensor(out=ft, in0=a, in1=b, op=ALU.mult), inc=True)
                        ubank_free[bk] = dv
                        pl.wait("scalar", "edve", dv)
                        ff_st[fslot] = pl.store(
                            inter[sp_ * 512 + isub * 128: sp_ * 512 + (isub + 1) * 128,
                                  tb * 512:(tb + 1) * 512], ft, ch=f"fs{fslot}")
            span_used[slot] = pl.n("epe")

    emit_dequant(10 ** 6)
    pl.barrier()

    def wdtile(ic):
        if ic < 4:
            return wdT03[:, ic * H:(ic + 1) * H]
        if ic < 6:
            return wdT45[:, (ic - 4) * H:(ic - 4 + 1) * H]
        return wdT6[:, :]

    acc_dve = {}
    out_st = [0, 0]
    prev_evac = {0: 0, 1: 0}
    prev_pp_pe = 0
    for pp in range(16):
        pl.wait("sync", "epe", prev_pp_pe)
        ilv = 0
        for ic in range(7):
            row = pp * 896 + ic * 128
            ilv = pl.load(i16[:, ic * 1024:(ic + 1) * 1024], inter[row:row + 128, :], ch="i16")
        wlv = 0
        for ic in range(7):
            wlv = pl.load(wdtile(ic), d_sc2[pp * 7 + ic, :, :], transpose=True, ch="wdt")
        pl.wait("tensor", "i16", ilv)
        pl.wait("tensor", "wdt", wlv)
        for hs in range(8):
            for half in range(2):
                tcs = range(4) if half == 0 else range(4, 8)
                pl.wait("tensor", "edve", prev_evac[half])
                lastpe = 0
                for ic in range(7):
                    rhs = wdtile(ic)[:, hs * 512:(hs + 1) * 512]
                    for tc in tcs:
                        lhsT = i16[:, ic * 1024 + tc * 128: ic * 1024 + (tc + 1) * 128]
                        lastpe = pl.mm(pb[tc][:], lhsT, rhs, ic == 0, ic == 6,
                                       inc=(tc == max(tcs)))
                pe_done = lastpe
                for tc in tcs:
                    aslice = acc[:, tc * H + hs * 512: tc * H + (hs + 1) * 512]
                    pl.wait("vector", "epe", pe_done)
                    if pp == 0:
                        bslot = tc % 2
                        prevb = acc_dve.get((hs, tc - 2), 0) if tc >= 2 else acc_dve.get((hs - 1, tc + 6), 0)
                        pl.wait("sync", "edve", prevb)
                        bv = pl.load(ph7x[:, bslot * 512:(bslot + 1) * 512],
                                     base[tc * 128:(tc + 1) * 128, hs * 512:(hs + 1) * 512], ch=f"pb7{bslot}")
                        pl.wait("vector", f"pb7{bslot}", bv)
                        dv = pl.dve(lambda e, o=aslice, a=pb[tc][:], b=ph7x[:, bslot * 512:(bslot + 1) * 512]:
                                    e.tensor_tensor(out=o, in0=a, in1=b, op=ALU.add), inc=True)
                    elif pp < 15:
                        pl.wait("vector", "edve", acc_dve[(hs, tc)])
                        dv = pl.dve(lambda e, o=aslice, a=pb[tc][:]:
                                    e.tensor_tensor(out=o, in0=a, in1=o, op=ALU.add), inc=True)
                    else:
                        pl.wait("vector", "edve", acc_dve[(hs, tc)])
                        oslot = tc % 2
                        pl.wait("vector", f"os{oslot}", out_st[oslot])
                        ot = ph7x[:, oslot * 512:(oslot + 1) * 512]
                        dv = pl.dve(lambda e, ot=ot, a=pb[tc][:], b=aslice:
                                    e.tensor_tensor(out=ot, in0=a, in1=b, op=ALU.add), inc=True)
                        pl.wait("scalar", "edve", dv)
                        out_st[oslot] = pl.store(out_t[tc * 128:(tc + 1) * 128, hs * 512:(hs + 1) * 512], ot,
                                                 ch=f"os{oslot}")
                    acc_dve[(hs, tc)] = dv
                prev_evac[half] = pl.n("edve")
        prev_pp_pe = pl.n("epe")

    pl.barrier(final=True)

    with nc.Block() as block:
        @block.sync
        def _(sync):
            for f in pl.ops["sync"]:
                f(sync)

        @block.scalar
        def _(scalar):
            for f in pl.ops["scalar"]:
                f(scalar)

        @block.vector
        def _(vector):
            for f in pl.ops["vector"]:
                f(vector)

        @block.tensor
        def _(tensor):
            for f in pl.ops["tensor"]:
                f(tensor)

        @block.gpsimd
        def _(gpsimd):
            for f in pl.ops["gpsimd"]:
                f(gpsimd)

    es.close()
    return nc


_CACHE = {}


def _host_constants(o, seq):
    jA = 4 + o
    jB = 3 - o
    pos = np.concatenate([np.arange(jA * 512, (jA + 1) * 512),
                          np.arange(jB * 512, (jB + 1) * 512)]).astype(np.float64)
    inv = 1.0 / (10000.0 ** (np.arange(0, HD, 2, dtype=np.float64) / HD))
    ang = pos[None, :] * inv[(np.arange(128) % 64)][:, None]
    cosq = np.cos(ang).astype(ml_dtypes.bfloat16)
    sq = np.sin(ang)
    sinq = np.empty((128, 1024), dtype=np.float64)
    sinq[0:64] = -sq[0:64]
    sinq[64:128] = sq[64:128]
    sinq = sinq.astype(ml_dtypes.bfloat16)
    posk = np.arange(S, dtype=np.float64)
    angk = posk[None, :] * inv[(np.arange(64) % 32)][:, None]
    cosk = np.cos(angk).astype(ml_dtypes.bfloat16)
    sk = np.sin(angk)
    sinkp = np.empty((64, S), dtype=np.float64)
    sinkp[0:32] = -sk[0:32]
    sinkp[32:64] = sk[32:64]
    sinkp = sinkp.astype(ml_dtypes.bfloat16)
    masks = np.zeros((128, 32, 512), dtype=np.float32)
    for idx in range(32):
        if idx < 16:
            k = 16 + idx
            j = 4 + o
        else:
            k = idx - 16
            j = 3 - o
        qpos = j * 512 + np.arange(512)[None, :]
        kpos = k * 128 + np.arange(128)[:, None]
        masks[:, idx, :] = (kpos <= qpos).astype(np.float32)
    masks = masks.reshape(128, 32 * 512).astype(ml_dtypes.bfloat16)
    consts = np.tile(np.array(CONST_VALS, dtype=np.float32)[None, :], (128, 1))
    ident = np.eye(128, dtype=np.float32)
    esel = np.zeros((4, 512), dtype=np.float32)
    for sub in range(4):
        esel[sub, sub * 128:(sub + 1) * 128] = 1.0
    e32 = np.zeros((128, 4096), dtype=ml_dtypes.bfloat16)
    for g in range(32):
        e32[g, g * 128:(g + 1) * 128] = 1.0
    zpad = np.zeros((II, 96), dtype=ml_dtypes.bfloat16)
    return dict(cosq=cosq, sinq=sinq, cosk=cosk, sinkp=sinkp, masks=masks,
                consts=consts, ident=ident, esel=esel, e32=e32, zpad=zpad)


def kernel(L0_attn_hidden_states, L0_attn_Wq, L0_attn_Wkr, L0_attn_Wdk,
           L0_attn_Wupk, L0_attn_Wupv, L0_attn_Wo,
           L0_ffn_gate_q, L0_ffn_gate_s, L0_ffn_up_q, L0_ffn_up_s,
           L0_ffn_down_q, L0_ffn_down_s):
    x = np.ascontiguousarray(np.asarray(L0_attn_hidden_states, dtype=np.float32))
    if "nc" not in _CACHE:
        _CACHE["nc"] = build_program()
    nc = _CACHE["nc"]

    shared = {
        "wq": np.ascontiguousarray(np.asarray(L0_attn_Wq, dtype=np.float32)),
        "wkr": np.ascontiguousarray(np.asarray(L0_attn_Wkr, dtype=np.float32)),
        "wdk": np.ascontiguousarray(np.asarray(L0_attn_Wdk, dtype=np.float32)),
        "wupk": np.ascontiguousarray(np.asarray(L0_attn_Wupk, dtype=np.float32)),
        "wupv": np.ascontiguousarray(np.asarray(L0_attn_Wupv, dtype=np.float32)),
        "wo": np.ascontiguousarray(np.asarray(L0_attn_Wo, dtype=np.float32)),
        "gq": np.ascontiguousarray(np.asarray(L0_ffn_gate_q, dtype=np.int32)),
        "gs": np.ascontiguousarray(np.asarray(L0_ffn_gate_s, dtype=np.float32)),
        "uq": np.ascontiguousarray(np.asarray(L0_ffn_up_q, dtype=np.int32)),
        "us": np.ascontiguousarray(np.asarray(L0_ffn_up_s, dtype=np.float32)),
        "dq": np.ascontiguousarray(np.asarray(L0_ffn_down_q, dtype=np.int32)),
        "ds": np.ascontiguousarray(np.asarray(L0_ffn_down_s, dtype=np.float32)),
    }
    in_maps = []
    for c in range(8):
        o = c % 4
        seq = c // 4
        jA = 4 + o
        jB = 3 - o
        m = dict(shared)
        m["xseq"] = np.ascontiguousarray(x[seq])
        m["xq"] = np.ascontiguousarray(
            np.concatenate([x[seq, jA * 512:(jA + 1) * 512], x[seq, jB * 512:(jB + 1) * 512]], axis=0))
        m.update(_host_constants(o, seq))
        in_maps.append(m)

    res = run_bass_kernel_spmd(nc, in_maps, core_ids=list(range(8)))
    out = np.empty((NB, S, H), dtype=np.float32)
    for c in range(8):
        o = c % 4
        seq = c // 4
        jA = 4 + o
        jB = 3 - o
        r = np.asarray(res.results[c]["out"])
        out[seq, jA * 512:(jA + 1) * 512] = r[0:512]
        out[seq, jB * 512:(jB + 1) * 512] = r[512:1024]
    return out
